# version 35
# baseline (speedup 1.0000x reference)
import numpy as np
import ml_dtypes
from contextlib import ExitStack

import concourse.bass as bass
import concourse.mybir as mybir
from concourse.bass_utils import run_bass_kernel_spmd
from concourse.alu_op_type import AluOpType as OP

F32 = mybir.dt.float32
BF16 = mybir.dt.bfloat16
AF = mybir.ActivationFunctionType
AX = mybir.AxisListType

NCORES = 8
D = 1024
E3 = 3 * D
NCH = 8
T_CORE = 4096
T_T = 512
NTILES = T_CORE // T_T
NPC = T_T // 128
EPS_LN = 1e-5
EPS_AT = 1e-3

bfdt = ml_dtypes.bfloat16


class Sched:
    def __init__(self):
        self.ops = []

    def op(self, engine, fn, deps=(), ev=None, sem=None, inc=1, drain=False):
        self.ops.append((engine, fn, tuple(deps), ev, sem, inc, drain))

    def resolve(self):
        counts = {}
        events = {}
        for (engine, fn, deps, ev, sem, inc, drain) in self.ops:
            counts[sem] = counts.get(sem, 0) + inc
            if ev is not None:
                assert ev not in events, f"duplicate event {ev}"
                events[ev] = (sem, counts[sem])
        return events

    def emit(self, engines, sems, events):
        watermark = {}
        for (engine, fn, deps, ev, sem, inc, drain) in self.ops:
            eng = engines[engine]
            for dep in deps:
                dsem, dval = events[dep]
                key = (engine, dsem)
                if watermark.get(key, 0) < dval:
                    eng.wait_ge(sems[dsem], dval)
                    watermark[key] = dval
            insts = fn(eng)
            if insts is None:
                insts = []
            elif not isinstance(insts, (list, tuple)):
                insts = [insts]
            if inc and insts:
                if sem in ("s_x0", "s_x1", "s_o", "s_w", "s_w0", "s_w1", "s_w2", "s_w3", "s_st", "s_ld", "s_sm",
                           "s_st0", "s_st1", "s_st2", "s_st3", "s_st4", "s_st5", "s_st6", "s_st7"):
                    assert inc == 16 * len(insts), (sem, inc, len(insts))
                    for it in insts:
                        it.then_inc(sems[sem], 16)
                else:
                    insts[-1].then_inc(sems[sem], inc)
            if drain and engine in ("vector", "scalar", "gpsimd") and insts:
                eng.drain()


def build_nc(trunc=None):
    nc = bass.Bass("TRN2", target_bir_lowering=False)

    x_hbm = nc.dram_tensor("x", [T_CORE, D], F32, kind="ExternalInput").ap()
    wqkvT = nc.dram_tensor("wqkvT", [D, E3], F32, kind="ExternalInput").ap()
    wprojT = nc.dram_tensor("wprojT", [D, D], F32, kind="ExternalInput").ap()
    wffn = nc.dram_tensor("wffn", [D, D], F32, kind="ExternalInput").ap()
    a_in = nc.dram_tensor("a_in", [D, 16], F32, kind="ExternalInput").ap()
    b_in = nc.dram_tensor("b_in", [16, D], F32, kind="ExternalInput").ap()
    bqkv_in = nc.dram_tensor("bqkv", [128, 24], F32, kind="ExternalInput").ap()
    bpr_in = nc.dram_tensor("bproj_rep", [1, D], F32, kind="ExternalInput").ap()
    bfr_in = nc.dram_tensor("bffn_rep", [1, D], F32, kind="ExternalInput").ap()
    srep_in = nc.dram_tensor("scale_rep", [128, 1], F32, kind="ExternalInput").ap()
    ident_in = nc.dram_tensor("ident", [128, 128], BF16, kind="ExternalInput").ap()
    selc_in = nc.dram_tensor("selc", [128, NCH * 16], BF16, kind="ExternalInput").ap()
    sel2_in = nc.dram_tensor("sel2", [16, NCH * 128], BF16, kind="ExternalInput").ap()
    mask_in = nc.dram_tensor("mask", [128, NCORES], F32, kind="ExternalInput").ap()
    out_hbm = nc.dram_tensor("out", [T_CORE, D], F32, kind="ExternalOutput").ap()
    kcs_hbm = nc.dram_tensor("kcs_hbm", [128, NCH, T_CORE], BF16).ap()
    kvcs_hbm = nc.dram_tensor("kvcs_hbm", [128, NCH, T_CORE], BF16).ap()
    q1_hbm = nc.dram_tensor("q1_hbm", [128, NCH, T_CORE], BF16).ap()
    kcs_flat = kcs_hbm.rearrange("p j c -> p (j c)")
    kvcs_flat = kvcs_hbm.rearrange("p j c -> p (j c)")
    q1_flat = q1_hbm.rearrange("p j c -> p (j c)")
    cc_in_t = nc.dram_tensor("cc_in", [128, 16], F32)
    cc_out_t = nc.dram_tensor("cc_out", [NCORES * 128, 16], F32)
    cc_in = cc_in_t.ap()
    cc_out = cc_out_t.ap()

    stack = ExitStack()
    stack.__enter__()

    def sb(name, shape, dtype):
        return stack.enter_context(nc.sbuf_tensor(name, shape, dtype))

    def ps(name, shape, dtype):
        return stack.enter_context(nc.psum_tensor(name, shape, dtype))

    wq_sb = [sb(f"wq{j}", [128, E3], BF16) for j in range(NCH)]
    wp_sb = [wq_sb[j][:, 0:D] for j in range(NCH)]
    wf_sb = [wq_sb[j][:, D:2 * D] for j in range(NCH)]
    a_sb = [wq_sb[j][:, 2 * D:2 * D + 16] for j in range(NCH)]
    b_sb0 = wq_sb[0][0:16, 2 * D + 16:2 * D + 16 + 512]
    b_sb1 = wq_sb[1][0:16, 2 * D + 16:2 * D + 16 + 512]
    ident_sb = sb("identb", [128, 128], BF16)
    selc_sb = sb("selcb", [128, NCH * 16], BF16)
    sel2_sb = sb("sel2b", [16, NCH * 128], BF16)
    bqs_sb = sb("bqss", [128, 24], F32)
    bqraw_sb = sb("bqraw", [128, 24], F32)
    bprow_sb = sb("bprow", [1, D], BF16)
    bfrow_sb = sb("bfrow", [1, D], BF16)
    onesrow_sb = sb("onesrow", [1, 128], BF16)
    srep_sb = sb("sreps", [128, 1], F32)
    mask_sb = sb("masks", [128, NCORES], F32)
    ones_sb = sb("oness", [128, T_T], BF16)
    soff_sb = ones_sb[:, 256:256 + NCH * 16]
    totals_sb = sb("totalss", [128, 16], F32)
    allg_sb = None
    off_sb = [sb(f"offs{i}", [128, 16], F32) for i in range(2)]
    stat_sb = sb("stats", [128, 80], F32)
    qrow_sb = None
    qcol_sb = sb("qcols", [128, 4], F32)
    carry_sb = sb("carrybuf", [128, 16], F32)
    consts_sb = sb("constsb", [128, 5], F32)

    x_sb = [sb(f"xbuf{i}", [128, NPC * D], F32) for i in range(2)]
    cs_base = nc.sbuf_base
    kcs_sb = sb("kcsbuf", [128, NCH * T_T], BF16)
    kvcs_sb = sb("kvcsbuf", [128, NCH * T_T], BF16)
    ffn_sb = nc.alloc_sbuf_tensor_at("ffnalias", [128, NCH * T_T], F32,
                                     offset=cs_base).ap()
    q1_sb = sb("q1buf", [128, NCH * T_T], BF16)

    AR = NCH * T_T
    arena_bf_base = nc.sbuf_base
    arena_bf = sb("arenabf", [128, 5 * AR], BF16)
    xln_sb = arena_bf[:, 0 * AR:1 * AR]
    xlnT_sb = arena_bf[:, 1 * AR:2 * AR]
    k1_sb = arena_bf[:, 2 * AR:3 * AR]
    kv_sb = arena_bf[:, 3 * AR:4 * AR]
    d_sb = arena_bf[:, 0 * AR:1 * AR]
    y_sb = arena_bf[:, 2 * AR:3 * AR]
    tn_sb = arena_bf[:, 1 * AR:2 * AR]
    rbc_sb = arena_bf[:, 2 * AR:3 * AR]
    xl3_sb = arena_bf[:, 3 * AR:4 * AR]
    xl3T_sb = arena_bf[:, 4 * AR:5 * AR]

    arena_f32_base = nc.sbuf_base
    arena_f32 = sb("arenaf32", [128, 2 * AR], F32)
    allg_sb = nc.alloc_sbuf_tensor_at("allgs", [128, NCORES * 16], F32,
                                      offset=arena_f32_base + (AR + 3 * D) * 4).ap()
    ev_extra = nc.alloc_sbuf_tensor_at("evextra", [128, 4 * T_T], F32,
                                       offset=arena_bf_base + 4 * AR * 2).ap()
    ev_sb = [arena_f32[:, i * T_T:(i + 1) * T_T] for i in range(8)] + \
            [ev_extra[:, i * T_T:(i + 1) * T_T] for i in range(4)]
    att2_base = nc.sbuf_base
    att2_sb = sb("att2buf", [128, NCH * T_T], F32)
    qrow_sb = nc.alloc_sbuf_tensor_at("qrows", [1, 264], F32,
                                      offset=att2_base + 3 * D * 4).ap()
    wstage2_f32 = att2_sb[:, 0:D]
    att_sb2 = [arena_f32[:, 0 * AR:1 * AR], att2_sb[:]]
    wstage = arena_f32[:, 1 * AR:1 * AR + E3]
    x2_sb = arena_f32[:, 1 * AR:2 * AR]

    rbt_sb = sb("rbtbuf", [16, T_T], F32)
    rbt2_sb = sb("rbt2buf", [16, T_T], F32)
    rb_sb = sb("rbbuf", [16, T_T], BF16)
    xa_sb = sb("xabuf", [128, NPC * 16], BF16)
    xaT_sb = wq_sb[2][0:16, 2 * D + 16:2 * D + 16 + NPC * 128]
    qtmp_sb = sb("qtmp2", [128, D], BF16)[:]

    tr_ps = [ps(f"trps{i}", [128, 2 * T_T], BF16) for i in range(2)]
    mm_ps = [ps(f"mmps{i}", [128, 512], F32) for i in range(6)]

    print("sbuf remaining:", nc.sbuf_bytes_remaining() if callable(getattr(nc, "sbuf_bytes_remaining", None)) else "?")

    S = Sched()

    qkv_steps = []
    st = 0
    for j in range(NCH):
        qkv_steps.append(("k", j, 8 + j, st % 6)); st += 1
        qkv_steps.append(("v", j, 16 + j, st % 6)); st += 1
        qkv_steps.append(("q", j, j, st % 6)); st += 1

    S.op("scalar", lambda e: [
        e.dma_start(out=bqraw_sb[:], in_=bqkv_in[:, :]),
        e.dma_start(out=srep_sb[:], in_=srep_in[:, :]),
        e.dma_start(out=mask_sb[:], in_=mask_in[:, :]),
        e.dma_start(out=ident_sb[:], in_=ident_in[:, :]),
        e.dma_start(out=selc_sb[:], in_=selc_in[:, :]),
        e.dma_start(out=sel2_sb[:], in_=sel2_in[:, :]),
    ], ev="w_small", sem="s_w", inc=16 * 6, drain=False)

    S.op("vector", lambda e: e.tensor_scalar(out=bqs_sb[:, 0:16], in0=bqraw_sb[:, 0:16],
                                             scalar1=srep_sb[:, 0:1], scalar2=None, op0=OP.mult),
         deps=["w_small"], sem="s_ve")
    S.op("vector", lambda e: e.tensor_copy(out=bqs_sb[:, 16:24], in_=bqraw_sb[:, 16:24]),
         ev="biasdone", sem="s_ve", drain=True)
    S.op("gpsimd", lambda e: e.memset(ones_sb[:], 1.0), ev="onesdone", sem="s_gp", drain=True)
    S.op("gpsimd", lambda e: [e.memset(consts_sb[:, 0:1], EPS_LN),
                              e.memset(consts_sb[:, 1:2], 0.0),
                              e.memset(consts_sb[:, 2:3], -EPS_AT),
                              e.memset(consts_sb[:, 3:4], 192.0),
                              e.memset(consts_sb[:, 4:5], EPS_AT)][-1],
         ev="constsdone", sem="s_gp", drain=True)

    wqstg2 = att2_sb[:, 0:E3]
    def emit_wqstage(j0, j1):
        for j in range(j0, j1):
            deps = ["w_small"] if j < 2 else [("wqconv", j - 2)]
            stg = wstage if j % 2 == 0 else wqstg2
            S.op("scalar" if j % 2 == 0 else "gpsimd",
                 (lambda jj, st: lambda e: e.dma_start(
                     out=st[:, 0:E3], in_=wqkvT[jj * 128:(jj + 1) * 128, :]))(j, stg),
                 deps=deps, ev=("wqload", j), sem="s_w" if j % 2 == 0 else "s_sm",
                 inc=16, drain=False)
            S.op("scalar", (lambda jj, st: lambda e: e.activation(
                out=wq_sb[jj][:, 0:2048], in_=st[:, 0:2048], func=AF.Identity,
                bias=consts_sb[:, 1:2], scale=srep_sb[:, 0:1]))(j, stg),
                deps=[("wqload", j), "constsdone"], sem="s_ac")
            S.op("scalar", (lambda jj, st: lambda e: e.copy(
                out=wq_sb[jj][:, 2048:3072], in_=st[:, 2048:3072]))(j, stg),
                ev=("wqconv", j), sem="s_ac")

    bank_free = {}
    slot_free = {}
    trb_free = {}

    def emit_A1(t):
        deps = [] if t < 2 else [("xdoneA", t - 2)]
        S.op("sync", (lambda tt: lambda e: e.dma_start(
            out=x_sb[tt % 2][:].rearrange("p (i c) -> p i c", i=NPC),
            in_=x_hbm[tt * T_T:(tt + 1) * T_T, :].rearrange("(i p) c -> p i c", p=128)))(t),
            deps=deps, ev=("xA", t), sem=f"s_x{t % 2}", inc=16, drain=False)
        for i in range(NPC):
            S.op("vector", (lambda tt, ii: lambda e: e.bn_stats(
                out=stat_sb[:, 32 + 12 * (ii % 2):32 + 12 * (ii % 2) + 6],
                in_=x_sb[tt % 2][:, ii * D:ii * D + 512]))(t, i),
                deps=[("xA", t)], sem="s_ve")
            S.op("vector", (lambda tt, ii: lambda e: e.bn_stats(
                out=stat_sb[:, 32 + 12 * (ii % 2) + 6:32 + 12 * (ii % 2) + 12],
                in_=x_sb[tt % 2][:, ii * D + 512:(ii + 1) * D]))(t, i),
                sem="s_ve", drain=True)
            S.op("vector", (lambda tt, ii: lambda e: e.bn_aggr(
                out=stat_sb[:, 2 * ii:2 * ii + 2],
                in_=stat_sb[:, 32 + 12 * (ii % 2):32 + 12 * (ii % 2) + 12]))(t, i),
                sem="s_ve", drain=True)
        S.op("vector", lambda e: e.tensor_scalar(
            out=stat_sb[:, 8:12], in0=stat_sb[:, 0:8:2], scalar1=-1.0, scalar2=None,
            op0=OP.mult),
            ev=("bn1", t), sem="s_ve", drain=True)
        S.op("scalar", lambda e: e.activation(out=stat_sb[:, 12:16], in_=stat_sb[:, 1:8:2],
                                              func=AF.Ln, bias=consts_sb[:, 0:1], scale=1.0),
             deps=[("bn1", t)] + (["constsdone"] if t == 0 else []), sem="s_ac", drain=True)
        S.op("scalar", lambda e: e.activation(out=stat_sb[:, 12:16], in_=stat_sb[:, 12:16],
                                              func=AF.Exp, bias=consts_sb[:, 1:2], scale=-0.5),
             ev=("r1", t), sem="s_ac")
        for i in range(NPC):
            deps = [("r1", t)]
            if t >= 1 and i == 0:
                deps.append(("trA", t - 1, NCH - 1))
            S.op("vector", (lambda tt, ii: lambda e: e.tensor_scalar(
                out=xln_sb[:, ii * D:(ii + 1) * D], in0=x_sb[tt % 2][:, ii * D:(ii + 1) * D],
                scalar1=stat_sb[:, 8 + ii:9 + ii], scalar2=stat_sb[:, 12 + ii:13 + ii],
                op0=OP.add, op1=OP.mult))(t, i),
                deps=deps, ev=("xln", t, i) if i < NPC - 1 else ("xdoneA", t), sem="s_ve")

    def emit_A2(t):
        for g in range(NCH // 2):
            deps = [("xdoneA", t)] + trb_free.get(g % 2, [])
            if t == 0 and g == 0:
                deps.append("w_small")
            S.op("tensor", (lambda tt, gg: lambda e: [
                e.transpose(out=tr_ps[gg % 2][:, h * T_T + i2 * 128:h * T_T + (i2 + 1) * 128],
                            in_=xln_sb[:, i2 * D + (2 * gg + h) * 128:i2 * D + (2 * gg + h + 1) * 128],
                            identity=ident_sb[:])
                for h in range(2) for i2 in range(NPC)])(t, g),
                deps=deps, ev=("trA", t, 2 * g + 1), sem="s_te")
            S.op("scalar", (lambda tt, gg: lambda e: e.copy(
                out=xlnT_sb[:, 2 * gg * T_T:(2 * gg + 2) * T_T], in_=tr_ps[gg % 2][:]))(t, g),
                deps=[("trA", t, 2 * g + 1)], ev=("xlnTev", t, 2 * g + 1), sem="s_ac")
            trb_free[g % 2] = [("xlnTev", t, 2 * g + 1)]

    def emit_A3(t):
        pending_scans = []
        for step, (kind, j, m, bank) in enumerate(qkv_steps):
            if kind == "k" and len(pending_scans) >= 1:
                pending_scans.pop(0)()
            deps = [("xlnTev", t, NCH - 1)]
            if t == 0 and step == 0:
                deps += [("wqconv", NCH - 1), "biasdone"]
            deps += bank_free.get(bank, [])
            S.op("tensor", (lambda tt, mm, bb: lambda e: [
                e.matmul(out=mm_ps[bb][:],
                         lhsT=wq_sb[k][:, mm * 128:(mm + 1) * 128],
                         rhs=xlnT_sb[:, k * T_T:(k + 1) * T_T],
                         start=(k == 0), stop=(k == NCH - 1)) for k in range(NCH)])(t, m, bank),
                deps=deps, ev=("qkvmm", t, step), sem="s_te")
            par = j % 3
            col = m
            if kind in ("k", "q"):
                zslot, rslot, eslot = par, 3 + par, 6 + par
                S.op("vector", (lambda zs, bb, cc: lambda e: e.tensor_scalar(
                    out=ev_sb[zs][:], in0=mm_ps[bb][:],
                    scalar1=bqs_sb[:, cc:cc + 1], scalar2=0.0,
                    op0=OP.add, op1=OP.min))(zslot, bank, col),
                    deps=[("qkvmm", t, step)] + slot_free.get(zslot, []),
                    ev=("zmin", t, step), sem="s_ve")
                S.op("scalar", (lambda rs, bb, cc: lambda e: e.activation(
                    out=ev_sb[rs][:], in_=mm_ps[bb][:], func=AF.Relu,
                    bias=bqs_sb[:, cc:cc + 1], scale=1.0))(rslot, bank, col),
                    deps=[("zmin", t, step)] + slot_free.get(rslot, []),
                    ev=("relu", t, step), sem="s_ac")
                S.op("scalar", (lambda es, zs: lambda e: e.activation(
                    out=ev_sb[es][:], in_=ev_sb[zs][:], func=AF.Exp,
                    bias=consts_sb[:, 1:2], scale=1.0))(eslot, zslot),
                    deps=[("zmin", t, step)] + slot_free.get(eslot, []),
                    ev=("expv", t, step), sem="s_ac")
                dst = k1_sb if kind == "k" else q1_sb
                S.op("gpsimd", (lambda es, rs, dd, jj: lambda e: e.tensor_tensor(
                    out=dd[:, jj * T_T:(jj + 1) * T_T], in0=ev_sb[es][:], in1=ev_sb[rs][:],
                    op=OP.add))(eslot, rslot, dst, j),
                    deps=[("expv", t, step), ("relu", t, step)] +
                         ([("spill", t - 1, j)] if (kind == "q" and t >= 1) else []) +
                         ([("scan_k", t - 1, j)] if (kind == "k" and t >= 1) else []),
                    ev=("add1", t, step), sem="s_gp", drain=(kind == "k"))
                bank_free[bank] = [("relu", t, step)]
                slot_free[zslot] = [("expv", t, step)]
                slot_free[rslot] = [("add1", t, step)]
                slot_free[eslot] = [("add1", t, step)]
            else:
                vslot = 9 + par
                S.op("scalar", (lambda vs, bb, cc: lambda e: e.activation(
                    out=ev_sb[vs][:], in_=mm_ps[bb][:], func=AF.Identity,
                    bias=bqs_sb[:, cc:cc + 1], scale=1.0))(vslot, bank, col),
                    deps=[("qkvmm", t, step)] + slot_free.get(vslot, []),
                    ev=("vev", t, step), sem="s_ac")
                S.op("gpsimd", (lambda vs, jj: lambda e: e.tensor_tensor(
                    out=kv_sb[:, jj * T_T:(jj + 1) * T_T],
                    in0=k1_sb[:, jj * T_T:(jj + 1) * T_T], in1=ev_sb[vs][:],
                    op=OP.mult))(vslot, j),
                    deps=[("vev", t, step), ("add1", t, step - 1)] +
                         ([("scan_kv", t - 1, j)] if t >= 1 else []),
                    ev=("kv", t, j), sem="s_gp", drain=False)
                bank_free[bank] = [("vev", t, step)]
                slot_free[vslot] = [("kv", t, j)]
                def mk_scans(tt, jj, stp):
                    def emit():
                        S.op("vector", (lambda: lambda e: e.tensor_tensor_scan(
                            out=kcs_sb[:, jj * T_T:(jj + 1) * T_T], data0=ones_sb[:],
                            data1=k1_sb[:, jj * T_T:(jj + 1) * T_T],
                            initial=(0.0 if tt == 0 else carry_sb[:, jj:jj + 1]),
                            op0=OP.mult, op1=OP.add))(),
                            deps=[("add1", tt, stp - 1), "onesdone"] +
                                 ([("spill", tt - 1, jj)] if tt >= 1 else []) +
                                 ([("carry", tt - 1)] if tt >= 1 else []),
                            ev=("scan_k", tt, jj), sem="s_ve")
                        S.op("vector", (lambda: lambda e: e.tensor_tensor_scan(
                            out=kvcs_sb[:, jj * T_T:(jj + 1) * T_T], data0=ones_sb[:],
                            data1=kv_sb[:, jj * T_T:(jj + 1) * T_T],
                            initial=(0.0 if tt == 0 else carry_sb[:, 8 + jj:9 + jj]),
                            op0=OP.mult, op1=OP.add))(),
                            deps=[("kv", tt, jj)],
                            ev=("scan_kv", tt, jj), sem="s_ve", drain=(jj == NCH - 1))
                        if tt < NTILES - 1:
                         S.op("sync", (lambda: lambda e: [
                            e.dma_start(out=kcs_flat[:, jj * T_CORE + tt * T_T:jj * T_CORE + (tt + 1) * T_T],
                                        in_=kcs_sb[:, jj * T_T:(jj + 1) * T_T]),
                            e.dma_start(out=kvcs_flat[:, jj * T_CORE + tt * T_T:jj * T_CORE + (tt + 1) * T_T],
                                         in_=kvcs_sb[:, jj * T_T:(jj + 1) * T_T]),
                            e.dma_start(out=q1_flat[:, jj * T_CORE + tt * T_T:jj * T_CORE + (tt + 1) * T_T],
                                        in_=q1_sb[:, jj * T_T:(jj + 1) * T_T])])(),
                             deps=[("scan_kv", tt, jj), ("add1", tt, 3 * jj + 2)],
                             ev=("spill", tt, jj), sem=f"s_st{jj}", inc=48, drain=False)
                    return emit
                pending_scans.append(mk_scans(t, j, step))
        for em in pending_scans:
            em()
        pending_scans = []
        S.op("vector", (lambda tt: lambda e: e.tensor_copy(
            out=carry_sb[:, 0:8],
            in_=kcs_sb[:, T_T - 1::T_T]))(t),
            deps=[("scan_k", t, NCH - 1)], sem="s_ve", drain=True)
        S.op("vector", (lambda tt: lambda e: e.tensor_copy(
            out=carry_sb[:, 8:16],
            in_=kvcs_sb[:, T_T - 1::T_T]))(t),
            deps=[("scan_kv", t, NCH - 1)], ev=("carry", t), sem="s_ve", drain=True)

    def emit_wfstat_load(j):
        slot = j % 3
        deps = [("wqconv", NCH - 1)] if j < 3 else [("wfstat", j - 3)]
        S.op("sync", (lambda jj, sl: lambda e: e.dma_start(
            out=wstage[:, sl * D:(sl + 1) * D], in_=wffn[jj * 128:(jj + 1) * 128, :]))(j, slot),
            deps=deps, ev=("wfload", j), sem=f"s_w{slot}", inc=16, drain=False)

    def emit_wfstat(j):
        slot = j % 3
        if j == 0:
            S.op("vector", (lambda sl: lambda e: e.tensor_reduce(
                out=qcol_sb[:, 0:1], in_=wstage[:, sl * D:(sl + 1) * D], axis=AX.X,
                op=OP.max, apply_absolute_value=True))(slot),
                deps=[("wfload", 0)], ev=("wfstat", 0), sem="s_ve", drain=True)
        else:
            S.op("vector", (lambda jj, sl: lambda e: e.tensor_reduce(
                out=qcol_sb[:, 3:4], in_=wstage[:, sl * D:(sl + 1) * D], axis=AX.X,
                op=OP.max, apply_absolute_value=True))(j, slot),
                deps=[("wfload", j)], sem="s_ve", drain=True)
            S.op("vector", (lambda jj: lambda e: e.tensor_tensor(
                out=qcol_sb[:, 0:1], in0=qcol_sb[:, 0:1], in1=qcol_sb[:, 3:4],
                op=OP.max))(j),
                ev=("wfstat", j), sem="s_ve", drain=True)

    def emit_qscale():
        S.op("gpsimd", lambda e: e.dma_start(out=qrow_sb[0:1, 0:128], in_=qcol_sb[:, 0:1]),
             deps=[("wfstat", NCH - 1)], ev="qrowdma", sem="s_sm", inc=16, drain=False)
        S.op("vector", lambda e: e.tensor_reduce(out=qrow_sb[0:1, 128:129],
                                                 in_=qrow_sb[0:1, 0:128],
                                                 axis=AX.X, op=OP.max),
             deps=["qrowdma"], sem="s_ve", drain=True)
        S.op("vector", lambda e: e.reciprocal(out=qrow_sb[0:1, 129:130],
                                              in_=qrow_sb[0:1, 128:129]), sem="s_ve", drain=True)
        S.op("vector", lambda e: e.tensor_scalar(out=qrow_sb[0:1, 130:131],
                                                 in0=qrow_sb[0:1, 129:130],
                                                 scalar1=7.0, scalar2=None, op0=OP.mult),
             sem="s_ve", drain=True)
        S.op("vector", lambda e: e.tensor_scalar(out=qrow_sb[0:1, 131:132],
                                                 in0=qrow_sb[0:1, 128:129],
                                                 scalar1=1.0 / 7.0, scalar2=None, op0=OP.mult),
             sem="s_ve", drain=True)
        S.op("vector", lambda e: e.tensor_scalar(out=qrow_sb[0:1, 132:260],
                                                 in0=qrow_sb[0:1, 0:128],
                                                 scalar1=0.0, scalar2=qrow_sb[0:1, 130:131],
                                                 op0=OP.mult, op1=OP.add),
             ev="qfill1", sem="s_ve", drain=True)
        S.op("gpsimd", lambda e: e.dma_start(out=qcol_sb[:, 1:2], in_=qrow_sb[0:1, 132:260]),
             deps=["qfill1"], ev="qdma1", sem="s_sm", inc=16, drain=False)
        S.op("vector", lambda e: e.tensor_scalar(out=qrow_sb[0:1, 132:260],
                                                 in0=qrow_sb[0:1, 0:128],
                                                 scalar1=0.0, scalar2=qrow_sb[0:1, 131:132],
                                                 op0=OP.mult, op1=OP.add),
             deps=["qdma1"], ev="qfill2", sem="s_ve", drain=True)
        S.op("gpsimd", lambda e: e.dma_start(out=qcol_sb[:, 2:3], in_=qrow_sb[0:1, 132:260]),
             deps=["qfill2"], ev="qscale", sem="s_sm", inc=16, drain=False)
    def emit_wfq_load(j):
        slot = j % 4
        deps = ["qscale"] if j < 4 else [("wfqc", j - 4)]
        S.op("scalar", (lambda jj, sl: lambda e: e.dma_start(
            out=att2_sb[:, sl * D:(sl + 1) * D], in_=wffn[jj * 128:(jj + 1) * 128, :]))(j, slot),
            deps=deps, ev=("wfload2", j), sem=f"s_w{slot}", inc=16, drain=False)

    def emit_wfq_convert():
        S.op("vector", lambda e: e.tensor_scalar(
            out=qcol_sb[:, 3:4], in0=qcol_sb[:, 2:3], scalar1=-192.0, scalar2=None,
            op0=OP.mult), deps=["qscale"], ev="n192sq", sem="s_ve", drain=True)
        for j in range(NCH):
            slot = j % 4
            deps = [("wfload2", j), "n192sq"]
            if j == 0:
                deps.append(("wpconv", NCH - 1))
            S.op("scalar", (lambda jj, sl: lambda e: e.activation(
                out=qtmp_sb, in_=att2_sb[:, sl * D:(sl + 1) * D], func=AF.Identity,
                bias=consts_sb[:, 3:4], scale=qcol_sb[:, 1:2]))(j, slot),
                deps=deps, sem="s_ac", drain=True)
            S.op("scalar", (lambda jj: lambda e: e.activation(
                out=wf_sb[jj], in_=qtmp_sb, func=AF.Identity,
                bias=qcol_sb[:, 3:4], scale=qcol_sb[:, 2:3]))(j),
                ev=("wfqc", j), sem="s_ac", drain=True)
            if j + 4 < NCH:
                emit_wfq_load(j + 4)
        S.op("scalar", lambda e: None, deps=[("wfqc", NCH - 1)], ev=("wfq", NCH - 1),
             sem="s_ac", inc=0, drain=False)
    emit_A1(0)
    emit_wqstage(0, 4)
    emit_A2(0)
    emit_A1(1)
    emit_wqstage(4, NCH)
    for j in range(3):
        emit_wfstat_load(j)
    for t in range(NTILES):
        emit_A3(t)
        ntl = max(1, NTILES // 2)
        if t < ntl:
            for j in range(t * NCH // ntl, (t + 1) * NCH // ntl):
                emit_wfstat(j)
                if j + 3 < NCH:
                    emit_wfstat_load(j + 3)
        if t == ntl and NTILES > 1:
            emit_qscale()
            for j in range(min(4, NCH)):
                emit_wfq_load(j)
        if t + 1 < NTILES:
            emit_A2(t + 1)
        if t + 2 < NTILES:
            emit_A1(t + 2)

    WQFREE = ("qkvmm", NTILES - 1, 23)
    for j in range(NCH):
        slot = j % 3
        deps = [("wfstat", NCH - 1)] if j < 3 else [("wpconv", j - 3)]
        S.op("scalar", (lambda jj, sl: lambda e: e.dma_start(
            out=wstage[:, sl * D:(sl + 1) * D], in_=wprojT[jj * 128:(jj + 1) * 128, :]))(j, slot),
            deps=deps, ev=("wpload", j), sem=f"s_w{slot}", inc=16, drain=False)
        S.op("scalar", (lambda jj, sl: lambda e: e.copy(
            out=wp_sb[jj], in_=wstage[:, sl * D:(sl + 1) * D]))(j, slot),
            deps=[("wpload", j), WQFREE], ev=("wpconv", j), sem="s_ac", drain=True)
    S.op("scalar", lambda e: e.dma_start(
        out=wstage[:, 0:NCH * 16].rearrange("p (j r) -> p j r", j=NCH),
        in_=a_in.rearrange("(j p) r -> p j r", p=128)),
        deps=[("wpconv", NCH - 1)], ev="aload", sem="s_w", inc=16, drain=False)
    for j in range(NCH):
        S.op("scalar", (lambda jj: lambda e: e.copy(
            out=a_sb[jj], in_=wstage[:, jj * 16:(jj + 1) * 16]))(j),
            deps=["aload"], ev=("aconv", j) if j == NCH - 1 else None, sem="s_ac",
            drain=(j == NCH - 1))
    S.op("gpsimd", lambda e: e.memset(onesrow_sb[:], 1.0), ev="onesrowdone", sem="s_gp", drain=True)
    S.op("scalar", lambda e: [
        e.dma_start(out=wstage[0:16, 0:D], in_=b_in[:, :]),
        e.dma_start(out=wstage[0:1, D:2 * D], in_=bpr_in[:, :]),
        e.dma_start(out=wstage[0:1, 2 * D:3 * D], in_=bfr_in[:, :])],
        deps=[("aconv", NCH - 1)], ev="bload", sem="s_w", inc=48, drain=False)
    S.op("scalar", lambda e: e.copy(out=b_sb0, in_=wstage[0:16, 0:512]),
         deps=["bload"], sem="s_ac", drain=True)
    S.op("scalar", lambda e: e.copy(out=b_sb1, in_=wstage[0:16, 512:D]),
         sem="s_ac", drain=True)
    S.op("scalar", lambda e: e.copy(out=bprow_sb[:], in_=wstage[0:1, D:2 * D]),
         ev="bconv", sem="s_ac", drain=True)
    S.op("scalar", lambda e: e.copy(out=bfrow_sb[:], in_=wstage[0:1, 2 * D:3 * D]),
         deps=["onesrowdone"], ev="browconv", sem="s_ac", drain=True)
    if NTILES <= 1:
        emit_qscale()
        for j in range(min(4, NCH)):
            emit_wfq_load(j)
    emit_wfq_convert()

    S.op("vector", lambda e: e.tensor_copy(out=totals_sb[:, 0:8], in_=kcs_sb[:, T_T - 1::T_T]),
         deps=[("scan_k", NTILES - 1, NCH - 1)], sem="s_ve")
    S.op("vector", lambda e: e.tensor_copy(out=totals_sb[:, 8:16], in_=kvcs_sb[:, T_T - 1::T_T]),
         deps=[("scan_kv", NTILES - 1, NCH - 1)], ev="totals", sem="s_ve")
    S.op("gpsimd", lambda e: e.dma_start(out=cc_in[:, :], in_=totals_sb[:]),
         deps=["totals"], ev="ccstage", sem="s_sm", inc=16, drain=False)
    S.op("gpsimd", lambda e: e.collective_compute(
        "AllGather", mybir.AluOpType.bypass, replica_groups=[list(range(NCORES))],
        ins=[cc_in.opt()], outs=[cc_out.opt()]),
        deps=["ccstage"], ev="ccdone", sem="s_cc", inc=1, drain=False)
    S.op("gpsimd", lambda e: [
        e.dma_start(out=allg_sb[:, s * 16:(s + 1) * 16], in_=cc_out[s * 128:(s + 1) * 128, :])
        for s in range(NCORES)],
        deps=["ccdone"], ev="allg", sem="s_sm", inc=16 * NCORES, drain=False)
    S.op("vector", lambda e: e.tensor_scalar(
        out=off_sb[0][:], in0=allg_sb[:, 0:16], scalar1=mask_sb[:, 0:1], scalar2=None,
        op0=OP.mult),
        deps=["allg", "w_small"], sem="s_ve", drain=True)
    for s in range(1, NCORES):
        S.op("vector", (lambda ss: lambda e: e.scalar_tensor_tensor(
            out=off_sb[ss % 2][:], in0=allg_sb[:, ss * 16:(ss + 1) * 16],
            scalar=mask_sb[:, ss:ss + 1], in1=off_sb[(ss - 1) % 2][:],
            op0=OP.mult, op1=OP.add))(s),
            ev="offsets" if s == NCORES - 1 else None, sem="s_ve", drain=True)
    offF = off_sb[(NCORES - 1) % 2]

    ORDER = ([NTILES - 1] + list(range(NTILES - 1))) if NTILES > 1 else [0]
    POS = {tt: p for p, tt in enumerate(ORDER)}
    def prv(tt, k=1):
        p = POS[tt] - k
        return ORDER[p] if p >= 0 else None
    def emit_S1a(t):
        deps = [("spill", NTILES - 2, j) for j in range(NCH)]
        deps += [("denmm", prv(t)), ("y", prv(t), NCH - 1)]
        S.op("sync", (lambda tt: lambda e: [
            e.dma_start(out=d_sb[:].rearrange("p (j c) -> p j c", j=NCH),
                        in_=kcs_hbm[:, :, tt * T_T:(tt + 1) * T_T]),
            e.dma_start(out=tn_sb[:].rearrange("p (j c) -> p j c", j=NCH),
                        in_=kvcs_hbm[:, :, tt * T_T:(tt + 1) * T_T]),
            e.dma_start(out=q1_sb[:].rearrange("p (j c) -> p j c", j=NCH),
                        in_=q1_hbm[:, :, tt * T_T:(tt + 1) * T_T])])(t),
            deps=deps, ev=("csload", t), sem="s_ld", inc=48, drain=False)
        deps = ([("x2", prv(t, 2), NPC - 1)] if prv(t, 2) is not None
                else [("xdoneA", NTILES - 1)])
        S.op("sync", (lambda tt: lambda e: e.dma_start(
            out=x_sb[tt % 2][:].rearrange("p (i c) -> p i c", i=NPC),
            in_=x_hbm[tt * T_T:(tt + 1) * T_T, :].rearrange("(i p) c -> p i c", p=128)))(t),
            deps=deps, ev=("xB", t), sem=f"s_x{t % 2}", inc=16, drain=False)
    def emit_S1c(t):
        res = (t == NTILES - 1)
        if res:
            for j in range(NCH):
                dd = [("add1", t, 3 * j + 2)]
                if j == 0:
                    dd += [("trA", NTILES - 1, NCH - 1), ("qkvmm", NTILES - 1, 23)]
                S.op("vector", (lambda jj: lambda e: e.tensor_tensor(
                    out=d_sb[:, jj * T_T:(jj + 1) * T_T], in0=kcs_sb[:, jj * T_T:(jj + 1) * T_T],
                    in1=q1_sb[:, jj * T_T:(jj + 1) * T_T], op=OP.mult))(j),
                    deps=dd, ev=("d", t, j), sem="s_ve")
                S.op("vector", (lambda jj: lambda e: e.tensor_tensor(
                    out=tn_sb[:, jj * T_T:(jj + 1) * T_T], in0=kvcs_sb[:, jj * T_T:(jj + 1) * T_T],
                    in1=q1_sb[:, jj * T_T:(jj + 1) * T_T], op=OP.mult))(j),
                    ev=("tnl", t, j), sem="s_ve", drain=(j == NCH - 1))
            for j in range(NCH):
                S.op("vector", (lambda jj: lambda e: e.tensor_scalar(
                    out=soff_sb[:, jj * 16:(jj + 1) * 16], in0=selc_sb[:, jj * 16:(jj + 1) * 16],
                    scalar1=offF[:, jj:jj + 1], scalar2=None, op0=OP.mult))(j),
                    deps=["offsets"], ev=("soff", t, j), sem="s_ve")
                S.op("vector", (lambda jj: lambda e: e.scalar_tensor_tensor(
                    out=tn_sb[:, jj * T_T:(jj + 1) * T_T], in0=q1_sb[:, jj * T_T:(jj + 1) * T_T],
                    scalar=offF[:, 8 + jj:9 + jj], in1=tn_sb[:, jj * T_T:(jj + 1) * T_T],
                    op0=OP.mult, op1=OP.add))(j),
                    ev=("tn", t, j), sem="s_ve")
            return
        for j in range(NCH):
            dd = [("csload", t), "offsets"]
            S.op("vector", (lambda jj: lambda e: e.scalar_tensor_tensor(
                out=d_sb[:, jj * T_T:(jj + 1) * T_T], in0=d_sb[:, jj * T_T:(jj + 1) * T_T],
                scalar=offF[:, jj:jj + 1], in1=q1_sb[:, jj * T_T:(jj + 1) * T_T],
                op0=OP.add, op1=OP.mult))(j),
                deps=dd, ev=("d", t, j), sem="s_ve")
            S.op("vector", (lambda jj: lambda e: e.scalar_tensor_tensor(
                out=tn_sb[:, jj * T_T:(jj + 1) * T_T], in0=tn_sb[:, jj * T_T:(jj + 1) * T_T],
                scalar=offF[:, 8 + jj:9 + jj], in1=q1_sb[:, jj * T_T:(jj + 1) * T_T],
                op0=OP.add, op1=OP.mult))(j),
                ev=("tn", t, j), sem="s_ve")
    def emit_S1d(t):
        deps = [("d", t, NCH - 1)] + bank_free.get(4, [])
        if t == ORDER[0]:
            deps += [("wpconv", NCH - 1)]
        if t == NTILES - 1:
            S.op("tensor", lambda e: [
                e.matmul(out=mm_ps[4][0:16, :],
                         lhsT=selc_sb[:, j2 * 16:(j2 + 1) * 16],
                         rhs=d_sb[:, j2 * T_T:(j2 + 1) * T_T],
                         start=(j2 == 0), stop=False) for j2 in range(NCH)],
                deps=deps, sem="s_te")
            S.op("tensor", lambda e: [
                e.matmul(out=mm_ps[4][0:16, :],
                         lhsT=soff_sb[:, j2 * 16:(j2 + 1) * 16],
                         rhs=q1_sb[:, j2 * T_T:(j2 + 1) * T_T],
                         start=False, stop=(j2 == NCH - 1)) for j2 in range(NCH)],
                deps=[("soff", t, NCH - 1)], ev=("denmm", t), sem="s_te")
        else:
            S.op("tensor", lambda e: [
                e.matmul(out=mm_ps[4][0:16, :],
                         lhsT=selc_sb[:, j2 * 16:(j2 + 1) * 16],
                         rhs=d_sb[:, j2 * T_T:(j2 + 1) * T_T],
                         start=(j2 == 0), stop=(j2 == NCH - 1)) for j2 in range(NCH)],
                deps=deps, ev=("denmm", t), sem="s_te")
        bank_free[4] = [("clip", t)]
        S.op("scalar", lambda e: e.activation(out=rbt_sb[:], in_=mm_ps[4][0:16, :],
                                              func=AF.Relu, bias=consts_sb[0:16, 2:3], scale=1.0),
             deps=[("denmm", t)], ev=("clip", t), sem="s_ac", drain=True)
        S.op("scalar", lambda e: e.activation(out=rbt2_sb[:], in_=rbt_sb[:],
                                              func=AF.Ln, bias=consts_sb[0:16, 4:5], scale=1.0),
             sem="s_ac", drain=True)
        S.op("scalar", lambda e: e.activation(out=rb_sb[:], in_=rbt2_sb[:],
                                              func=AF.Exp, bias=consts_sb[0:16, 1:2], scale=-1.0),
             deps=([("bcmm", prv(t), NCH - 1)] if prv(t) is not None else []),
             ev=("rb", t), sem="s_ac")
    def emit_S1b2(t):
        for j in range(NCH):
            bb = 4 + (j % 2)
            deps = [("rb", t)] + bank_free.get(bb, [])
            S.op("tensor", (lambda jj, p: lambda e: e.matmul(
                out=mm_ps[p][:], lhsT=sel2_sb[:, jj * 128:(jj + 1) * 128], rhs=rb_sb[:],
                start=True, stop=True))(j, bb),
                deps=deps, ev=("bcmm", t, j), sem="s_te")
            S.op("scalar", (lambda jj, p: lambda e: e.activation(
                out=rbc_sb[:, jj * T_T:(jj + 1) * T_T], in_=mm_ps[p][:], func=AF.Identity,
                bias=consts_sb[:, 1:2], scale=1.0))(j, bb),
                deps=[("bcmm", t, j)] +
                     ([("scan_k", NTILES - 1, NCH - 1), ("kv", NTILES - 1, NCH - 1)]
                      if (t == ORDER[0] and j == 0) else []) +
                     ([("projmm", prv(t), NPC - 1)] if (prv(t) is not None and j == 0) else []),
                ev=("rbc", t, j), sem="s_ac")
            bank_free[bb] = [("rbc", t, j)]
        for j in range(NCH):
            S.op("vector", (lambda jj: lambda e: e.tensor_tensor(
                out=y_sb[:, jj * T_T:(jj + 1) * T_T], in0=tn_sb[:, jj * T_T:(jj + 1) * T_T],
                in1=rbc_sb[:, jj * T_T:(jj + 1) * T_T], op=OP.mult))(j),
                deps=[("rbc", t, j), ("tn", t, j)],
                ev=("y", t, j), sem="s_ve")
        for i in range(NPC):
            b0, b1 = (0, 1) if i % 2 == 0 else (2, 3)
            deps = [("y", t, NCH - 1)] + bank_free.get(b0, []) + bank_free.get(b1, [])
            if t == 0 and i == 0:
                deps.append("browconv")
            S.op("tensor", (lambda ii, p0, p1: lambda e: [
                e.matmul(out=mm_ps[p0][:],
                         lhsT=y_sb[:, k * T_T + ii * 128:k * T_T + (ii + 1) * 128],
                         rhs=wp_sb[k][:, 0:512], start=(k == 0), stop=False)
                for k in range(NCH)] + [
                e.matmul(out=mm_ps[p0][:], lhsT=onesrow_sb[0:1, 0:128],
                         rhs=bprow_sb[0:1, 0:512], start=False, stop=True)] + [
                e.matmul(out=mm_ps[p1][:],
                         lhsT=y_sb[:, k * T_T + ii * 128:k * T_T + (ii + 1) * 128],
                         rhs=wp_sb[k][:, 512:1024], start=(k == 0), stop=False)
                for k in range(NCH)] + [
                e.matmul(out=mm_ps[p1][:], lhsT=onesrow_sb[0:1, 0:128],
                         rhs=bprow_sb[0:1, 512:1024], start=False, stop=True)])(i, b0, b1),
                deps=deps, ev=("projmm", t, i), sem="s_te")
            bank_free[b0] = [("projev", t, i)]
            bank_free[b1] = [("projev", t, i)]
            att = att_sb2[t % 2]
            S.op("scalar", (lambda ii, aa, p0: lambda e: e.copy(
                out=aa[:, ii * D:ii * D + 512], in_=mm_ps[p0][:]))(i, att, b0),
                deps=[("projmm", t, i)] +
                     ([("add1", NTILES - 1, 23)] if (t == 0 and i == 0) else []) +
                     ([("x3", prv(t, 2), NPC - 1)] if (prv(t, 2) is not None and i == 0) else []),
                sem="s_ac")
            S.op("scalar", (lambda ii, aa, p1: lambda e: e.copy(
                out=aa[:, ii * D + 512:(ii + 1) * D], in_=mm_ps[p1][:]))(i, att, b1),
                ev=("projev", t, i), sem="s_ac")

    def emit_S2a(t):
        att = att_sb2[t % 2]
        for i in range(NPC):
            S.op("vector", (lambda ii, aa: lambda e: e.bn_stats(
                out=stat_sb[:, 64 + 0:64 + 6], in_=aa[:, ii * D:ii * D + 512]))(i, att),
                deps=[("projev", t, i)], sem="s_ve")
            S.op("vector", (lambda ii, aa: lambda e: e.bn_stats(
                out=stat_sb[:, 64 + 6:64 + 12], in_=aa[:, ii * D + 512:(ii + 1) * D]))(i, att),
                sem="s_ve", drain=True)
            S.op("vector", (lambda ii: lambda e: e.bn_aggr(
                out=stat_sb[:, 16 + 2 * ii:16 + 2 * ii + 2],
                in_=stat_sb[:, 64:64 + 12]))(i),
                sem="s_ve", drain=True)
        S.op("vector", lambda e: e.tensor_scalar(
            out=stat_sb[:, 24:28], in0=stat_sb[:, 16:24:2], scalar1=-1.0, scalar2=None,
            op0=OP.mult), ev=("bn2", t), sem="s_ve", drain=True)
        S.op("scalar", lambda e: e.activation(out=stat_sb[:, 28:32], in_=stat_sb[:, 17:24:2],
                                              func=AF.Ln, bias=consts_sb[:, 0:1], scale=1.0),
             deps=[("bn2", t)], sem="s_ac", drain=True)
        S.op("scalar", lambda e: e.activation(out=stat_sb[:, 28:32], in_=stat_sb[:, 28:32],
                                              func=AF.Exp, bias=consts_sb[:, 1:2], scale=-0.5),
             ev=("r2", t), sem="s_ac")
        for i in range(NPC):
            S.op("vector", (lambda ii, aa: lambda e: e.tensor_scalar(
                out=ffn_sb[:, ii * D:(ii + 1) * D], in0=aa[:, ii * D:(ii + 1) * D],
                scalar1=stat_sb[:, 24 + ii:25 + ii], scalar2=stat_sb[:, 28 + ii:29 + ii],
                op0=OP.add, op1=OP.mult))(i, att),
                deps=[("r2", t)] +
                     ([("x3", prv(t), NPC - 1)] if (prv(t) is not None and i == 0) else []),
                ev=("ln2", t, i), sem="s_ve")
            deps_x2 = [("ln2", t, i)]
            if t != NTILES - 1:
                deps_x2.append(("xB", t))
            if prv(t) is not None:
                deps_x2.append(("ostore", prv(t)))
            else:
                deps_x2.append(("wfq", NCH - 1))
            S.op("vector", (lambda tt, ii: lambda e: e.tensor_tensor(
                out=x2_sb[:, ii * D:(ii + 1) * D], in0=x_sb[tt % 2][:, ii * D:(ii + 1) * D],
                in1=ffn_sb[:, ii * D:(ii + 1) * D], op=OP.add))(t, i),
                deps=deps_x2, ev=("x2", t, i), sem="s_ve", drain=True)
    def emit_S2b(t):
        att = att_sb2[t % 2]
        for i in range(NPC):
            S.op("vector", (lambda ii: lambda e: e.bn_stats(
                out=stat_sb[:, 64:64 + 6], in_=x2_sb[:, ii * D:ii * D + 512]))(i),
                deps=[("x2", t, i)], sem="s_ve")
            S.op("vector", (lambda ii: lambda e: e.bn_stats(
                out=stat_sb[:, 64 + 6:64 + 12], in_=x2_sb[:, ii * D + 512:(ii + 1) * D]))(i),
                sem="s_ve", drain=True)
            S.op("vector", (lambda ii: lambda e: e.bn_aggr(
                out=stat_sb[:, 32 + 2 * ii:32 + 2 * ii + 2],
                in_=stat_sb[:, 64:64 + 12]))(i),
                sem="s_ve", drain=True)
        S.op("vector", lambda e: e.tensor_scalar(
            out=stat_sb[:, 40:44], in0=stat_sb[:, 32:40:2], scalar1=-1.0, scalar2=None,
            op0=OP.mult), ev=("bn3", t), sem="s_ve", drain=True)
        S.op("scalar", lambda e: e.activation(out=stat_sb[:, 44:48], in_=stat_sb[:, 33:40:2],
                                              func=AF.Ln, bias=consts_sb[:, 0:1], scale=1.0),
             deps=[("bn3", t)], sem="s_ac", drain=True)
        S.op("scalar", lambda e: e.activation(out=stat_sb[:, 44:48], in_=stat_sb[:, 44:48],
                                              func=AF.Exp, bias=consts_sb[:, 1:2], scale=-0.5),
             ev=("r3", t), sem="s_ac")
        for i in range(NPC):
            S.op("vector", (lambda ii: lambda e: e.tensor_scalar(
                out=xl3_sb[:, ii * D:(ii + 1) * D], in0=x2_sb[:, ii * D:(ii + 1) * D],
                scalar1=stat_sb[:, 40 + ii:41 + ii], scalar2=stat_sb[:, 44 + ii:45 + ii],
                op0=OP.add, op1=OP.mult))(i),
                deps=[("r3", t)] +
                     ([("scan_kv", NTILES - 1, NCH - 1)] if (t == ORDER[0] and i == 0) else []),
                ev=("ln3", t, i), sem="s_ve")
        for g in range(NCH // 2):
            deps = [("ln3", t, NPC - 1)] + trb_free.get(g % 2, [])
            S.op("tensor", (lambda gg: lambda e: [
                e.transpose(out=tr_ps[gg % 2][:, h * T_T + i2 * 128:h * T_T + (i2 + 1) * 128],
                            in_=xl3_sb[:, i2 * D + (2 * gg + h) * 128:i2 * D + (2 * gg + h + 1) * 128],
                            identity=ident_sb[:])
                for h in range(2) for i2 in range(NPC)])(g),
                deps=deps, ev=("tr3", t, 2 * g + 1), sem="s_te")
            S.op("scalar", (lambda gg: lambda e: e.copy(
                out=xl3T_sb[:, 2 * gg * T_T:(2 * gg + 2) * T_T], in_=tr_ps[gg % 2][:]))(g),
                deps=[("tr3", t, 2 * g + 1)] +
                     ([("wfq", NCH - 1)] if (t == ORDER[0] and g == 0) else []),
                ev=("xl3Tev", t, 2 * g + 1), sem="s_ac")
            trb_free[g % 2] = [("xl3Tev", t, 2 * g + 1)]
        deps = [("xl3Tev", t, NCH - 1)] + bank_free.get(5, [])
        if t == ORDER[0]:
            deps.append(("aconv", NCH - 1))
        S.op("tensor", lambda e: [
            e.matmul(out=mm_ps[5][:, ii2 * 16:(ii2 + 1) * 16],
                     lhsT=xl3T_sb[:, k * T_T + ii2 * 128:k * T_T + (ii2 + 1) * 128],
                     rhs=a_sb[k][:], start=(k == 0), stop=(k == NCH - 1))
            for ii2 in range(NPC) for k in range(NCH)],
            deps=deps, ev=("xamm", t), sem="s_te")
        bank_free[5] = [("xaev", t)]
        S.op("scalar", lambda e: e.copy(out=xa_sb[:], in_=mm_ps[5][:, 0:NPC * 16]),
             deps=[("xamm", t)], ev=("xaev", t), sem="s_ac")
        deps_xat = [("xaev", t)] + trb_free.get(0, [])
        S.op("tensor", lambda e: [
            e.transpose(out=tr_ps[0][0:16, ii2 * 128:(ii2 + 1) * 128],
                        in_=xa_sb[:, ii2 * 16:(ii2 + 1) * 16], identity=ident_sb[:])
            for ii2 in range(NPC)],
            deps=deps_xat, ev=("xaT", t), sem="s_te")
        trb_free[0] = [("xaTev", t)]
        S.op("scalar", lambda e: e.copy(out=xaT_sb, in_=tr_ps[0][0:16, 0:NPC * 128]),
             deps=[("xaT", t)], ev=("xaTev", t), sem="s_ac")
        for i in range(NPC):
            f0, f1 = (2, 3) if i % 2 == 0 else (0, 1)
            deps_f = [("xaTev", t)] + bank_free.get(f0, []) + bank_free.get(f1, [])
            if t == ORDER[0] and i == 0:
                deps_f += ["bconv", ("wfq", NCH - 1)]
            S.op("tensor", (lambda ii, p0, p1: lambda e: [
                e.matmul(out=mm_ps[p0][:],
                         lhsT=xl3T_sb[:, k * T_T + ii * 128:k * T_T + (ii + 1) * 128],
                         rhs=wf_sb[k][:, 0:512], start=(k == 0), stop=False)
                for k in range(NCH)] + [
                e.matmul(out=mm_ps[p0][:], lhsT=xaT_sb[:, ii * 128:(ii + 1) * 128],
                         rhs=b_sb0, start=False, stop=False)] + [
                e.matmul(out=mm_ps[p0][:], lhsT=onesrow_sb[0:1, 0:128],
                         rhs=bfrow_sb[0:1, 0:512], start=False, stop=True)] + [
                e.matmul(out=mm_ps[p1][:],
                         lhsT=xl3T_sb[:, k * T_T + ii * 128:k * T_T + (ii + 1) * 128],
                         rhs=wf_sb[k][:, 512:1024], start=(k == 0), stop=False)
                for k in range(NCH)] + [
                e.matmul(out=mm_ps[p1][:], lhsT=xaT_sb[:, ii * 128:(ii + 1) * 128],
                         rhs=b_sb1, start=False, stop=False)] + [
                e.matmul(out=mm_ps[p1][:], lhsT=onesrow_sb[0:1, 0:128],
                         rhs=bfrow_sb[0:1, 512:1024], start=False, stop=True)])(i, f0, f1),
                deps=deps_f, ev=("ffnmm", t, i), sem="s_te")
            bank_free[f0] = [("ffnev", t, i)]
            bank_free[f1] = [("ffnev", t, i)]
            S.op("scalar", (lambda ii, p0: lambda e: e.copy(
                out=ffn_sb[:, ii * D:ii * D + 512], in_=mm_ps[p0][:]))(i, f0),
                deps=[("ffnmm", t, i)] +
                     ([("x2", t, NPC - 1)] if i == 0 else []),
                sem="s_ac")
            S.op("scalar", (lambda ii, p1: lambda e: e.copy(
                out=ffn_sb[:, ii * D + 512:(ii + 1) * D], in_=mm_ps[p1][:]))(i, f1),
                ev=("ffnev", t, i), sem="s_ac")
        for i in range(NPC):
            S.op("vector", (lambda ii: lambda e: e.bn_stats(
                out=stat_sb[:, 64:64 + 6], in_=ffn_sb[:, ii * D:ii * D + 512]))(i),
                deps=[("ffnev", t, i)], sem="s_ve")
            S.op("vector", (lambda ii: lambda e: e.bn_stats(
                out=stat_sb[:, 64 + 6:64 + 12], in_=ffn_sb[:, ii * D + 512:(ii + 1) * D]))(i),
                sem="s_ve", drain=True)
            S.op("vector", (lambda ii: lambda e: e.bn_aggr(
                out=stat_sb[:, 48 + 2 * ii:48 + 2 * ii + 2],
                in_=stat_sb[:, 64:64 + 12]))(i),
                sem="s_ve", drain=True)
        S.op("vector", lambda e: e.tensor_scalar(
            out=stat_sb[:, 56:60], in0=stat_sb[:, 48:56:2], scalar1=-1.0, scalar2=None,
            op0=OP.mult), ev=("bn4", t), sem="s_ve", drain=True)
        S.op("scalar", lambda e: e.activation(out=stat_sb[:, 60:64], in_=stat_sb[:, 49:56:2],
                                              func=AF.Ln, bias=consts_sb[:, 0:1], scale=1.0),
             deps=[("bn4", t)], sem="s_ac", drain=True)
        S.op("scalar", lambda e: e.activation(out=stat_sb[:, 60:64], in_=stat_sb[:, 60:64],
                                              func=AF.Exp, bias=consts_sb[:, 1:2], scale=-0.5),
             ev=("r4", t), sem="s_ac")
        for i in range(NPC):
            S.op("vector", (lambda ii, aa: lambda e: e.tensor_scalar(
                out=aa[:, ii * D:(ii + 1) * D], in0=ffn_sb[:, ii * D:(ii + 1) * D],
                scalar1=stat_sb[:, 56 + ii:57 + ii], scalar2=stat_sb[:, 60 + ii:61 + ii],
                op0=OP.add, op1=OP.mult))(i, att),
                deps=[("r4", t)], ev=("ln4", t, i), sem="s_ve")
            S.op("gpsimd", (lambda ii, aa: lambda e: e.tensor_tensor(
                out=x2_sb[:, ii * D:(ii + 1) * D], in0=x2_sb[:, ii * D:(ii + 1) * D],
                in1=aa[:, ii * D:(ii + 1) * D], op=OP.add))(i, att),
                deps=[("ln4", t, i)], ev=("x3", t, i), sem="s_gp")
        for i3 in range(NPC):
            S.op("sync", (lambda tt, ii: lambda e: e.dma_start(
                out=out_hbm[tt * T_T + ii * 128:tt * T_T + (ii + 1) * 128, :],
                in_=x2_sb[:, ii * D:(ii + 1) * D]))(t, i3),
                deps=[("x3", t, i3)],
                ev=("ostore", t) if i3 == NPC - 1 else None, sem="s_o", inc=16,
                drain=False)

    emit_S1c(ORDER[0])
    emit_S1d(ORDER[0])
    emit_S1b2(ORDER[0])
    if NTILES > 1:
        emit_S1a(ORDER[1])
        emit_S1c(ORDER[1])
    for p, t in enumerate(ORDER):
        if p + 1 < NTILES:
            emit_S1d(ORDER[p + 1])
        emit_S2a(t)
        if p + 1 < NTILES:
            emit_S1b2(ORDER[p + 1])
        if p + 2 < NTILES:
            emit_S1a(ORDER[p + 2])
            emit_S1c(ORDER[p + 2])
        emit_S2b(t)

    if trunc is not None:
        S.ops = S.ops[:trunc]
    events = S.resolve()
    engines = {}
    sems = {}
    for s in ["s_x0", "s_x1", "s_w", "s_w0", "s_w1", "s_w2", "s_w3", "s_st", "s_ld", "s_o",
              "s_cc", "s_te", "s_ve", "s_ac", "s_gp", "s_sm",
              "s_st0", "s_st1", "s_st2", "s_st3", "s_st4", "s_st5", "s_st6", "s_st7"]:
        sems[s] = stack.enter_context(nc.semaphore(s))
    block = stack.enter_context(nc.Block())

    def emit_engine(ename, eng):
        sub = Sched()
        sub.ops = [o for o in S.ops if o[0] == ename]
        sub.emit({ename: eng}, sems, events)
        if ename == "sync":
            final_t = (NTILES - 2) if NTILES > 1 else 0
            if ("ostore", final_t) in events:
                final_sem, final_val = events[("ostore", final_t)]
                eng.wait_ge(sems[final_sem], final_val)
            else:
                for sname in ("s_o", "s_x0", "s_x1", "s_st", "s_ld", "s_w", "s_sm"):
                    mx = 0
                    for (evv, (sm, val)) in events.items():
                        if sm == sname:
                            mx = max(mx, val)
                    if mx:
                        eng.wait_ge(sems[sname], mx)

    @block.sync
    def _(sync):
        emit_engine("sync", sync)

    @block.scalar
    def _(scalar):
        emit_engine("scalar", scalar)

    @block.vector
    def _(vector):
        emit_engine("vector", vector)

    @block.tensor
    def _(tensor):
        emit_engine("tensor", tensor)

    @block.gpsimd
    def _(gpsimd):
        emit_engine("gpsimd", gpsimd)

    stack.__exit__(None, None, None)
    return nc


_CACHE = {}


def kernel(L0_attn_x, L0_attn_Wqkv, L0_attn_bqkv, L0_attn_Wproj, L0_attn_bproj,
           L0_attn_scale, L0_ffn_kernel, L0_ffn_bias, L0_ffn_a, L0_ffn_b):
    x = np.ascontiguousarray(np.asarray(L0_attn_x, dtype=np.float32).reshape(2 * 16384, D))
    wqkvT = np.ascontiguousarray(np.asarray(L0_attn_Wqkv, np.float32).T)
    wprojT = np.ascontiguousarray(np.asarray(L0_attn_Wproj, np.float32).T)
    wffn = np.ascontiguousarray(np.asarray(L0_ffn_kernel, np.float32))
    a_np = np.ascontiguousarray(np.asarray(L0_ffn_a, np.float32))
    b_np = np.ascontiguousarray(np.asarray(L0_ffn_b, np.float32))
    bqkv = np.asarray(L0_attn_bqkv, np.float32)
    bproj = np.asarray(L0_attn_bproj, np.float32)
    bffn = np.asarray(L0_ffn_bias, np.float32)
    scale = np.float32(np.asarray(L0_attn_scale, np.float32))

    bqkv_t = np.ascontiguousarray(bqkv.reshape(24, 128).T)
    bpr_rep = np.ascontiguousarray(bproj[None, :])
    bfr_rep = np.ascontiguousarray(bffn[None, :])
    srep = np.full((128, 1), scale, np.float32)
    ident = np.eye(128, dtype=np.float32).astype(bfdt)
    selc = np.zeros((128, NCH * 16), np.float32)
    for j in range(NCH):
        selc[0:64, j * 16 + 2 * j] = 1.0
        selc[64:128, j * 16 + 2 * j + 1] = 1.0
    sel2 = np.zeros((16, NCH * 128), np.float32)
    for j in range(NCH):
        sel2[2 * j, j * 128:j * 128 + 64] = 1.0
        sel2[2 * j + 1, j * 128 + 64:(j + 1) * 128] = 1.0
    selc = selc.astype(bfdt)
    sel2 = sel2.astype(bfdt)

    key = "prog"
    if key not in _CACHE:
        _CACHE[key] = build_nc()
    nc = _CACHE[key]

    in_maps = []
    for c in range(NCORES):
        seq = c // 4
        pos = c % 4
        mask = np.zeros(NCORES, np.float32)
        for j in range(NCORES):
            if j // 4 == seq and (j % 4) < pos:
                mask[j] = 1.0
        mask_rep = np.ascontiguousarray(np.broadcast_to(mask[None, :], (128, NCORES)))
        in_maps.append({
            "x": np.ascontiguousarray(x[c * T_CORE:(c + 1) * T_CORE]),
            "wqkvT": wqkvT, "wprojT": wprojT, "wffn": wffn,
            "a_in": a_np, "b_in": b_np,
            "bqkv": bqkv_t, "bproj_rep": bpr_rep, "bffn_rep": bfr_rep,
            "scale_rep": srep, "ident": ident, "selc": selc, "sel2": sel2,
            "mask": mask_rep,
        })
    res = run_bass_kernel_spmd(nc, in_maps, core_ids=list(range(NCORES)))
    out = np.concatenate([r["out"] for r in res.results], axis=0)
    return out.reshape(2, 16384, D)


# version 36
# speedup vs baseline: 1.0483x; 1.0483x over previous
import numpy as np
import ml_dtypes
from contextlib import ExitStack

import concourse.bass as bass
import concourse.mybir as mybir
from concourse.bass_utils import run_bass_kernel_spmd
from concourse.alu_op_type import AluOpType as OP

F32 = mybir.dt.float32
BF16 = mybir.dt.bfloat16
AF = mybir.ActivationFunctionType
AX = mybir.AxisListType

NCORES = 8
D = 1024
E3 = 3 * D
NCH = 8
T_CORE = 4096
T_T = 512
NTILES = T_CORE // T_T
NPC = T_T // 128
EPS_LN = 1e-5
EPS_AT = 1e-3

bfdt = ml_dtypes.bfloat16


class Sched:
    def __init__(self):
        self.ops = []

    def op(self, engine, fn, deps=(), ev=None, sem=None, inc=1, drain=False):
        self.ops.append((engine, fn, tuple(deps), ev, sem, inc, drain))

    def resolve(self):
        counts = {}
        events = {}
        for (engine, fn, deps, ev, sem, inc, drain) in self.ops:
            counts[sem] = counts.get(sem, 0) + inc
            if ev is not None:
                assert ev not in events, f"duplicate event {ev}"
                events[ev] = (sem, counts[sem])
        return events

    def emit(self, engines, sems, events):
        watermark = {}
        for (engine, fn, deps, ev, sem, inc, drain) in self.ops:
            eng = engines[engine]
            for dep in deps:
                dsem, dval = events[dep]
                key = (engine, dsem)
                if watermark.get(key, 0) < dval:
                    eng.wait_ge(sems[dsem], dval)
                    watermark[key] = dval
            insts = fn(eng)
            if insts is None:
                insts = []
            elif not isinstance(insts, (list, tuple)):
                insts = [insts]
            if inc and insts:
                if sem in ("s_x0", "s_x1", "s_o", "s_w", "s_w0", "s_w1", "s_w2", "s_w3", "s_st", "s_ld", "s_sm",
                           "s_st0", "s_st1", "s_st2", "s_st3", "s_st4", "s_st5", "s_st6", "s_st7"):
                    assert inc == 16 * len(insts), (sem, inc, len(insts))
                    for it in insts:
                        it.then_inc(sems[sem], 16)
                else:
                    insts[-1].then_inc(sems[sem], inc)
            if drain and engine in ("vector", "scalar", "gpsimd") and insts:
                eng.drain()


def build_nc(trunc=None):
    nc = bass.Bass("TRN2", target_bir_lowering=False)

    x_hbm = nc.dram_tensor("x", [T_CORE, D], F32, kind="ExternalInput").ap()
    wqkvT = nc.dram_tensor("wqkvT", [D, E3], F32, kind="ExternalInput").ap()
    wprojT = nc.dram_tensor("wprojT", [D, D], F32, kind="ExternalInput").ap()
    wffn = nc.dram_tensor("wffn", [D, D], F32, kind="ExternalInput").ap()
    a_in = nc.dram_tensor("a_in", [D, 16], F32, kind="ExternalInput").ap()
    b_in = nc.dram_tensor("b_in", [16, D], F32, kind="ExternalInput").ap()
    bqkv_in = nc.dram_tensor("bqkv", [128, 24], F32, kind="ExternalInput").ap()
    bpr_in = nc.dram_tensor("bproj_rep", [1, D], F32, kind="ExternalInput").ap()
    bfr_in = nc.dram_tensor("bffn_rep", [1, D], F32, kind="ExternalInput").ap()
    srep_in = nc.dram_tensor("scale_rep", [128, 1], F32, kind="ExternalInput").ap()
    ident_in = nc.dram_tensor("ident", [128, 128], BF16, kind="ExternalInput").ap()
    selc_in = nc.dram_tensor("selc", [128, NCH * 16], BF16, kind="ExternalInput").ap()
    sel2_in = nc.dram_tensor("sel2", [16, NCH * 128], BF16, kind="ExternalInput").ap()
    mask_in = nc.dram_tensor("mask", [128, NCORES], F32, kind="ExternalInput").ap()
    out_hbm = nc.dram_tensor("out", [T_CORE, D], F32, kind="ExternalOutput").ap()
    kcs_hbm = nc.dram_tensor("kcs_hbm", [128, NCH, T_CORE], BF16).ap()
    kvcs_hbm = nc.dram_tensor("kvcs_hbm", [128, NCH, T_CORE], BF16).ap()
    q1_hbm = nc.dram_tensor("q1_hbm", [128, NCH, T_CORE], BF16).ap()
    kcs_flat = kcs_hbm.rearrange("p j c -> p (j c)")
    kvcs_flat = kvcs_hbm.rearrange("p j c -> p (j c)")
    q1_flat = q1_hbm.rearrange("p j c -> p (j c)")
    cc_in_t = nc.dram_tensor("cc_in", [128, 16], F32)
    cc_out_t = nc.dram_tensor("cc_out", [NCORES * 128, 16], F32)
    cc_in = cc_in_t.ap()
    cc_out = cc_out_t.ap()

    stack = ExitStack()
    stack.__enter__()

    def sb(name, shape, dtype):
        return stack.enter_context(nc.sbuf_tensor(name, shape, dtype))

    def ps(name, shape, dtype):
        return stack.enter_context(nc.psum_tensor(name, shape, dtype))

    wq_sb = [sb(f"wq{j}", [128, E3], BF16) for j in range(NCH)]
    wp_sb = [wq_sb[j][:, 0:D] for j in range(NCH)]
    wf_sb = [wq_sb[j][:, D:2 * D] for j in range(NCH)]
    a_sb = [wq_sb[j][:, 2 * D:2 * D + 16] for j in range(NCH)]
    b_sb0 = wq_sb[0][0:16, 2 * D + 16:2 * D + 16 + 512]
    b_sb1 = wq_sb[1][0:16, 2 * D + 16:2 * D + 16 + 512]
    ident_sb = sb("identb", [128, 128], BF16)
    selc_sb = sb("selcb", [128, NCH * 16], BF16)
    sel2_sb = sb("sel2b", [16, NCH * 128], BF16)
    bqs_sb = sb("bqss", [128, 24], F32)
    bqraw_sb = sb("bqraw", [128, 24], F32)
    bprow_sb = sb("bprow", [1, D], BF16)
    bfrow_sb = sb("bfrow", [1, D], BF16)
    onesrow_sb = sb("onesrow", [1, 128], BF16)
    srep_sb = sb("sreps", [128, 1], F32)
    mask_sb = sb("masks", [128, NCORES], F32)
    ones_sb = sb("oness", [128, T_T], BF16)
    soff_sb = ones_sb[:, 256:256 + NCH * 16]
    totals_sb = sb("totalss", [128, 16], F32)
    allg_sb = None
    off_sb = [sb(f"offs{i}", [128, 16], F32) for i in range(2)]
    stat_sb = sb("stats", [128, 80], F32)
    qrow_sb = None
    qcol_sb = sb("qcols", [128, 4], F32)
    carry_sb = sb("carrybuf", [128, 16], F32)
    consts_sb = sb("constsb", [128, 5], F32)

    x_sb = [sb(f"xbuf{i}", [128, NPC * D], F32) for i in range(2)]
    cs_base = nc.sbuf_base
    kcs_sb = sb("kcsbuf", [128, NCH * T_T], BF16)
    kvcs_sb = sb("kvcsbuf", [128, NCH * T_T], BF16)
    ffn_sb = nc.alloc_sbuf_tensor_at("ffnalias", [128, NCH * T_T], F32,
                                     offset=cs_base).ap()
    q1_sb = sb("q1buf", [128, NCH * T_T], BF16)

    AR = NCH * T_T
    arena_bf_base = nc.sbuf_base
    arena_bf = sb("arenabf", [128, 5 * AR], BF16)
    xln_sb = arena_bf[:, 0 * AR:1 * AR]
    xlnT_sb = arena_bf[:, 1 * AR:2 * AR]
    k1_sb = arena_bf[:, 2 * AR:3 * AR]
    kv_sb = arena_bf[:, 3 * AR:4 * AR]
    d_sb = arena_bf[:, 0 * AR:1 * AR]
    y_sb = arena_bf[:, 2 * AR:3 * AR]
    tn_sb = arena_bf[:, 1 * AR:2 * AR]
    rbc_sb = arena_bf[:, 2 * AR:3 * AR]
    xl3_sb = arena_bf[:, 3 * AR:4 * AR]
    xl3T_sb = arena_bf[:, 4 * AR:5 * AR]

    arena_f32_base = nc.sbuf_base
    arena_f32 = sb("arenaf32", [128, 2 * AR], F32)
    allg_sb = nc.alloc_sbuf_tensor_at("allgs", [128, NCORES * 16], F32,
                                      offset=arena_f32_base + (AR + 3 * D) * 4).ap()
    ev_extra = nc.alloc_sbuf_tensor_at("evextra", [128, 4 * T_T], F32,
                                       offset=arena_bf_base + 4 * AR * 2).ap()
    ev_sb = [arena_f32[:, i * T_T:(i + 1) * T_T] for i in range(8)] + \
            [ev_extra[:, i * T_T:(i + 1) * T_T] for i in range(4)]
    att2_base = nc.sbuf_base
    att2_sb = sb("att2buf", [128, NCH * T_T], F32)
    qrow_sb = nc.alloc_sbuf_tensor_at("qrows", [1, 264], F32,
                                      offset=att2_base + 3 * D * 4).ap()
    wstage2_f32 = att2_sb[:, 0:D]
    att_sb2 = [arena_f32[:, 0 * AR:1 * AR], att2_sb[:]]
    wstage = arena_f32[:, 1 * AR:1 * AR + E3]
    x2_sb = arena_f32[:, 1 * AR:2 * AR]

    rbt_sb = sb("rbtbuf", [16, T_T], F32)
    rbt2_sb = sb("rbt2buf", [16, T_T], F32)
    rb_sb = sb("rbbuf", [16, T_T], BF16)
    xa_sb = sb("xabuf", [128, NPC * 16], BF16)
    xaT_sb = wq_sb[2][0:16, 2 * D + 16:2 * D + 16 + NPC * 128]
    qtmp_sb = sb("qtmp2", [128, D], BF16)[:]

    tr_ps = [ps(f"trps{i}", [128, 2 * T_T], BF16) for i in range(2)]
    mm_ps = [ps(f"mmps{i}", [128, 512], F32) for i in range(6)]

    print("sbuf remaining:", nc.sbuf_bytes_remaining() if callable(getattr(nc, "sbuf_bytes_remaining", None)) else "?")

    S = Sched()

    qkv_steps = []
    st = 0
    for j in range(NCH):
        qkv_steps.append(("k", j, 8 + j, st % 6)); st += 1
        qkv_steps.append(("v", j, 16 + j, st % 6)); st += 1
        qkv_steps.append(("q", j, j, st % 6)); st += 1

    S.op("scalar", lambda e: [
        e.dma_start(out=bqraw_sb[:], in_=bqkv_in[:, :]),
        e.dma_start(out=srep_sb[:], in_=srep_in[:, :]),
        e.dma_start(out=mask_sb[:], in_=mask_in[:, :]),
        e.dma_start(out=ident_sb[:], in_=ident_in[:, :]),
        e.dma_start(out=selc_sb[:], in_=selc_in[:, :]),
        e.dma_start(out=sel2_sb[:], in_=sel2_in[:, :]),
    ], ev="w_small", sem="s_w", inc=16 * 6, drain=False)

    S.op("vector", lambda e: e.tensor_scalar(out=bqs_sb[:, 0:16], in0=bqraw_sb[:, 0:16],
                                             scalar1=srep_sb[:, 0:1], scalar2=None, op0=OP.mult),
         deps=["w_small"], sem="s_ve")
    S.op("vector", lambda e: e.tensor_copy(out=bqs_sb[:, 16:24], in_=bqraw_sb[:, 16:24]),
         ev="biasdone", sem="s_ve", drain=True)
    S.op("gpsimd", lambda e: e.memset(ones_sb[:], 1.0), ev="onesdone", sem="s_gp", drain=True)
    S.op("gpsimd", lambda e: [e.memset(consts_sb[:, 0:1], EPS_LN),
                              e.memset(consts_sb[:, 1:2], 0.0),
                              e.memset(consts_sb[:, 2:3], -EPS_AT),
                              e.memset(consts_sb[:, 3:4], 192.0),
                              e.memset(consts_sb[:, 4:5], EPS_AT)][-1],
         ev="constsdone", sem="s_gp", drain=True)

    wqstg2 = att2_sb[:, 0:E3]
    def emit_wqstage(j0, j1):
        for j in range(j0, j1):
            deps = ["w_small"] if j < 2 else [("wqconv", j - 2)]
            stg = wstage if j % 2 == 0 else wqstg2
            S.op("scalar" if j % 2 == 0 else "gpsimd",
                 (lambda jj, st: lambda e: e.dma_start(
                     out=st[:, 0:E3], in_=wqkvT[jj * 128:(jj + 1) * 128, :]))(j, stg),
                 deps=deps, ev=("wqload", j), sem="s_w" if j % 2 == 0 else "s_sm",
                 inc=16, drain=False)
            S.op("scalar", (lambda jj, st: lambda e: e.activation(
                out=wq_sb[jj][:, 0:2048], in_=st[:, 0:2048], func=AF.Identity,
                bias=consts_sb[:, 1:2], scale=srep_sb[:, 0:1]))(j, stg),
                deps=[("wqload", j), "constsdone"], sem="s_ac")
            S.op("scalar", (lambda jj, st: lambda e: e.copy(
                out=wq_sb[jj][:, 2048:3072], in_=st[:, 2048:3072]))(j, stg),
                ev=("wqconv", j), sem="s_ac")

    bank_free = {}
    slot_free = {}
    trb_free = {}

    def emit_A1(t):
        deps = [] if t < 2 else [("xdoneA", t - 2)]
        S.op("sync", (lambda tt: lambda e: e.dma_start(
            out=x_sb[tt % 2][:].rearrange("p (i c) -> p i c", i=NPC),
            in_=x_hbm[tt * T_T:(tt + 1) * T_T, :].rearrange("(i p) c -> p i c", p=128)))(t),
            deps=deps, ev=("xA", t), sem=f"s_x{t % 2}", inc=16, drain=False)
        for i in range(NPC):
            S.op("vector", (lambda tt, ii: lambda e: e.bn_stats(
                out=stat_sb[:, 32 + 12 * (ii % 2):32 + 12 * (ii % 2) + 6],
                in_=x_sb[tt % 2][:, ii * D:ii * D + 512]))(t, i),
                deps=[("xA", t)], sem="s_ve")
            S.op("vector", (lambda tt, ii: lambda e: e.bn_stats(
                out=stat_sb[:, 32 + 12 * (ii % 2) + 6:32 + 12 * (ii % 2) + 12],
                in_=x_sb[tt % 2][:, ii * D + 512:(ii + 1) * D]))(t, i),
                sem="s_ve", drain=True)
            S.op("vector", (lambda tt, ii: lambda e: e.bn_aggr(
                out=stat_sb[:, 2 * ii:2 * ii + 2],
                in_=stat_sb[:, 32 + 12 * (ii % 2):32 + 12 * (ii % 2) + 12]))(t, i),
                sem="s_ve", drain=True)
        S.op("vector", lambda e: e.tensor_scalar(
            out=stat_sb[:, 8:12], in0=stat_sb[:, 0:8:2], scalar1=-1.0, scalar2=None,
            op0=OP.mult),
            ev=("bn1", t), sem="s_ve", drain=True)
        S.op("scalar", lambda e: e.activation(out=stat_sb[:, 12:16], in_=stat_sb[:, 1:8:2],
                                              func=AF.Ln, bias=consts_sb[:, 0:1], scale=1.0),
             deps=[("bn1", t)] + (["constsdone"] if t == 0 else []), sem="s_ac", drain=True)
        S.op("scalar", lambda e: e.activation(out=stat_sb[:, 12:16], in_=stat_sb[:, 12:16],
                                              func=AF.Exp, bias=consts_sb[:, 1:2], scale=-0.5),
             ev=("r1", t), sem="s_ac")
        for i in range(NPC):
            deps = [("r1", t)]
            if t >= 1 and i == 0:
                deps.append(("trA", t - 1, NCH - 1))
            S.op("vector", (lambda tt, ii: lambda e: e.tensor_scalar(
                out=xln_sb[:, ii * D:(ii + 1) * D], in0=x_sb[tt % 2][:, ii * D:(ii + 1) * D],
                scalar1=stat_sb[:, 8 + ii:9 + ii], scalar2=stat_sb[:, 12 + ii:13 + ii],
                op0=OP.add, op1=OP.mult))(t, i),
                deps=deps, ev=("xln", t, i) if i < NPC - 1 else ("xdoneA", t), sem="s_ve")

    def emit_A2(t):
        for g in range(NCH // 2):
            deps = [("xdoneA", t)] + trb_free.get(g % 2, [])
            if t == 0 and g == 0:
                deps.append("w_small")
            S.op("tensor", (lambda tt, gg: lambda e: [
                e.transpose(out=tr_ps[gg % 2][:, h * T_T + i2 * 128:h * T_T + (i2 + 1) * 128],
                            in_=xln_sb[:, i2 * D + (2 * gg + h) * 128:i2 * D + (2 * gg + h + 1) * 128],
                            identity=ident_sb[:])
                for h in range(2) for i2 in range(NPC)])(t, g),
                deps=deps, ev=("trA", t, 2 * g + 1), sem="s_te")
            S.op("scalar", (lambda tt, gg: lambda e: e.copy(
                out=xlnT_sb[:, 2 * gg * T_T:(2 * gg + 2) * T_T], in_=tr_ps[gg % 2][:]))(t, g),
                deps=[("trA", t, 2 * g + 1)], ev=("xlnTev", t, 2 * g + 1), sem="s_ac")
            trb_free[g % 2] = [("xlnTev", t, 2 * g + 1)]

    def emit_A3(t):
        pending_scans = []
        for step, (kind, j, m, bank) in enumerate(qkv_steps):
            if kind == "k" and len(pending_scans) >= 1:
                pending_scans.pop(0)()
            deps = [("xlnTev", t, NCH - 1)]
            if t == 0 and step == 0:
                deps += [("wqconv", NCH - 1), "biasdone"]
            deps += bank_free.get(bank, [])
            S.op("tensor", (lambda tt, mm, bb: lambda e: [
                e.matmul(out=mm_ps[bb][:],
                         lhsT=wq_sb[k][:, mm * 128:(mm + 1) * 128],
                         rhs=xlnT_sb[:, k * T_T:(k + 1) * T_T],
                         start=(k == 0), stop=(k == NCH - 1)) for k in range(NCH)])(t, m, bank),
                deps=deps, ev=("qkvmm", t, step), sem="s_te")
            par = j % 3
            col = m
            if kind in ("k", "q"):
                zslot, rslot, eslot = par, 3 + par, 6 + par
                S.op("vector", (lambda zs, bb, cc: lambda e: e.tensor_scalar(
                    out=ev_sb[zs][:], in0=mm_ps[bb][:],
                    scalar1=bqs_sb[:, cc:cc + 1], scalar2=0.0,
                    op0=OP.add, op1=OP.min))(zslot, bank, col),
                    deps=[("qkvmm", t, step)] + slot_free.get(zslot, []),
                    ev=("zmin", t, step), sem="s_ve")
                S.op("scalar", (lambda rs, bb, cc: lambda e: e.activation(
                    out=ev_sb[rs][:], in_=mm_ps[bb][:], func=AF.Relu,
                    bias=bqs_sb[:, cc:cc + 1], scale=1.0))(rslot, bank, col),
                    deps=[("zmin", t, step)] + slot_free.get(rslot, []),
                    ev=("relu", t, step), sem="s_ac")
                S.op("scalar", (lambda es, zs: lambda e: e.activation(
                    out=ev_sb[es][:], in_=ev_sb[zs][:], func=AF.Exp,
                    bias=consts_sb[:, 1:2], scale=1.0))(eslot, zslot),
                    deps=[("zmin", t, step)] + slot_free.get(eslot, []),
                    ev=("expv", t, step), sem="s_ac")
                dst = k1_sb if kind == "k" else q1_sb
                S.op("gpsimd", (lambda es, rs, dd, jj: lambda e: e.tensor_tensor(
                    out=dd[:, jj * T_T:(jj + 1) * T_T], in0=ev_sb[es][:], in1=ev_sb[rs][:],
                    op=OP.add))(eslot, rslot, dst, j),
                    deps=[("expv", t, step), ("relu", t, step)] +
                         ([("spill", t - 1, j)] if (kind == "q" and t >= 1) else []) +
                         ([("scan_k", t - 1, j)] if (kind == "k" and t >= 1) else []),
                    ev=("add1", t, step), sem="s_gp", drain=(kind == "k"))
                bank_free[bank] = [("relu", t, step)]
                slot_free[zslot] = [("expv", t, step)]
                slot_free[rslot] = [("add1", t, step)]
                slot_free[eslot] = [("add1", t, step)]
            else:
                vslot = 9 + par
                S.op("scalar", (lambda vs, bb, cc: lambda e: e.activation(
                    out=ev_sb[vs][:], in_=mm_ps[bb][:], func=AF.Identity,
                    bias=bqs_sb[:, cc:cc + 1], scale=1.0))(vslot, bank, col),
                    deps=[("qkvmm", t, step)] + slot_free.get(vslot, []),
                    ev=("vev", t, step), sem="s_ac")
                S.op("gpsimd", (lambda vs, jj: lambda e: e.tensor_tensor(
                    out=kv_sb[:, jj * T_T:(jj + 1) * T_T],
                    in0=k1_sb[:, jj * T_T:(jj + 1) * T_T], in1=ev_sb[vs][:],
                    op=OP.mult))(vslot, j),
                    deps=[("vev", t, step), ("add1", t, step - 1)] +
                         ([("scan_kv", t - 1, j)] if t >= 1 else []),
                    ev=("kv", t, j), sem="s_gp", drain=False)
                bank_free[bank] = [("vev", t, step)]
                slot_free[vslot] = [("kv", t, j)]
                def mk_scans(tt, jj, stp):
                    def emit():
                        S.op("vector", (lambda: lambda e: e.tensor_tensor_scan(
                            out=kcs_sb[:, jj * T_T:(jj + 1) * T_T], data0=ones_sb[:],
                            data1=k1_sb[:, jj * T_T:(jj + 1) * T_T],
                            initial=(0.0 if tt == 0 else carry_sb[:, jj:jj + 1]),
                            op0=OP.mult, op1=OP.add))(),
                            deps=[("add1", tt, stp - 1), "onesdone"] +
                                 ([("spill", tt - 1, jj)] if tt >= 1 else []) +
                                 ([("carry", tt - 1)] if tt >= 1 else []),
                            ev=("scan_k", tt, jj), sem="s_ve")
                        S.op("vector", (lambda: lambda e: e.tensor_tensor_scan(
                            out=kvcs_sb[:, jj * T_T:(jj + 1) * T_T], data0=ones_sb[:],
                            data1=kv_sb[:, jj * T_T:(jj + 1) * T_T],
                            initial=(0.0 if tt == 0 else carry_sb[:, 8 + jj:9 + jj]),
                            op0=OP.mult, op1=OP.add))(),
                            deps=[("kv", tt, jj)],
                            ev=("scan_kv", tt, jj), sem="s_ve", drain=(jj == NCH - 1))
                        if tt < NTILES - 1:
                         S.op("sync", (lambda: lambda e: [
                            e.dma_start(out=kcs_flat[:, jj * T_CORE + tt * T_T:jj * T_CORE + (tt + 1) * T_T],
                                        in_=kcs_sb[:, jj * T_T:(jj + 1) * T_T]),
                            e.dma_start(out=kvcs_flat[:, jj * T_CORE + tt * T_T:jj * T_CORE + (tt + 1) * T_T],
                                         in_=kvcs_sb[:, jj * T_T:(jj + 1) * T_T]),
                            e.dma_start(out=q1_flat[:, jj * T_CORE + tt * T_T:jj * T_CORE + (tt + 1) * T_T],
                                        in_=q1_sb[:, jj * T_T:(jj + 1) * T_T])])(),
                             deps=[("scan_kv", tt, jj), ("add1", tt, 3 * jj + 2)],
                             ev=("spill", tt, jj), sem=f"s_st{jj}", inc=48, drain=False)
                    return emit
                pending_scans.append(mk_scans(t, j, step))
        for em in pending_scans:
            em()
        pending_scans = []
        S.op("vector", (lambda tt: lambda e: e.tensor_copy(
            out=carry_sb[:, 0:8],
            in_=kcs_sb[:, T_T - 1::T_T]))(t),
            deps=[("scan_k", t, NCH - 1)], sem="s_ve", drain=True)
        S.op("vector", (lambda tt: lambda e: e.tensor_copy(
            out=carry_sb[:, 8:16],
            in_=kvcs_sb[:, T_T - 1::T_T]))(t),
            deps=[("scan_kv", t, NCH - 1)], ev=("carry", t), sem="s_ve", drain=True)

    def emit_wfstat_load(j):
        slot = j % 3
        deps = [("wqconv", NCH - 1)] if j < 3 else [("wfstat", j - 3)]
        S.op("sync", (lambda jj, sl: lambda e: e.dma_start(
            out=wstage[:, sl * D:(sl + 1) * D], in_=wffn[jj * 128:(jj + 1) * 128, :]))(j, slot),
            deps=deps, ev=("wfload", j), sem=f"s_w{slot}", inc=16, drain=False)

    def emit_wfstat(j):
        slot = j % 3
        if j == 0:
            S.op("vector", (lambda sl: lambda e: e.tensor_reduce(
                out=qcol_sb[:, 0:1], in_=wstage[:, sl * D:(sl + 1) * D], axis=AX.X,
                op=OP.max, apply_absolute_value=True))(slot),
                deps=[("wfload", 0)], ev=("wfstat", 0), sem="s_ve", drain=True)
        else:
            S.op("vector", (lambda jj, sl: lambda e: e.tensor_reduce(
                out=qcol_sb[:, 3:4], in_=wstage[:, sl * D:(sl + 1) * D], axis=AX.X,
                op=OP.max, apply_absolute_value=True))(j, slot),
                deps=[("wfload", j)], sem="s_ve", drain=True)
            S.op("vector", (lambda jj: lambda e: e.tensor_tensor(
                out=qcol_sb[:, 0:1], in0=qcol_sb[:, 0:1], in1=qcol_sb[:, 3:4],
                op=OP.max))(j),
                ev=("wfstat", j), sem="s_ve", drain=True)

    def emit_qscale():
        S.op("gpsimd", lambda e: e.dma_start(out=qrow_sb[0:1, 0:128], in_=qcol_sb[:, 0:1]),
             deps=[("wfstat", NCH - 1)], ev="qrowdma", sem="s_sm", inc=16, drain=False)
        S.op("vector", lambda e: e.tensor_reduce(out=qrow_sb[0:1, 128:129],
                                                 in_=qrow_sb[0:1, 0:128],
                                                 axis=AX.X, op=OP.max),
             deps=["qrowdma"], sem="s_ve", drain=True)
        S.op("vector", lambda e: e.reciprocal(out=qrow_sb[0:1, 129:130],
                                              in_=qrow_sb[0:1, 128:129]), sem="s_ve", drain=True)
        S.op("vector", lambda e: e.tensor_scalar(out=qrow_sb[0:1, 130:131],
                                                 in0=qrow_sb[0:1, 129:130],
                                                 scalar1=7.0, scalar2=None, op0=OP.mult),
             sem="s_ve", drain=True)
        S.op("vector", lambda e: e.tensor_scalar(out=qrow_sb[0:1, 131:132],
                                                 in0=qrow_sb[0:1, 128:129],
                                                 scalar1=1.0 / 7.0, scalar2=None, op0=OP.mult),
             sem="s_ve", drain=True)
        S.op("vector", lambda e: e.tensor_scalar(out=qrow_sb[0:1, 132:260],
                                                 in0=qrow_sb[0:1, 0:128],
                                                 scalar1=0.0, scalar2=qrow_sb[0:1, 130:131],
                                                 op0=OP.mult, op1=OP.add),
             ev="qfill1", sem="s_ve", drain=True)
        S.op("gpsimd", lambda e: e.dma_start(out=qcol_sb[:, 1:2], in_=qrow_sb[0:1, 132:260]),
             deps=["qfill1"], ev="qdma1", sem="s_sm", inc=16, drain=False)
        S.op("vector", lambda e: e.tensor_scalar(out=qrow_sb[0:1, 132:260],
                                                 in0=qrow_sb[0:1, 0:128],
                                                 scalar1=0.0, scalar2=qrow_sb[0:1, 131:132],
                                                 op0=OP.mult, op1=OP.add),
             deps=["qdma1"], ev="qfill2", sem="s_ve", drain=True)
        S.op("gpsimd", lambda e: e.dma_start(out=qcol_sb[:, 2:3], in_=qrow_sb[0:1, 132:260]),
             deps=["qfill2"], ev="qscale", sem="s_sm", inc=16, drain=False)
    def emit_wfq_load(j):
        slot = j % 4
        deps = ["qscale"] if j < 4 else [("wfqc", j - 4)]
        S.op("scalar", (lambda jj, sl: lambda e: e.dma_start(
            out=att2_sb[:, sl * D:(sl + 1) * D], in_=wffn[jj * 128:(jj + 1) * 128, :]))(j, slot),
            deps=deps, ev=("wfload2", j), sem=f"s_w{slot}", inc=16, drain=False)

    def emit_wfq_convert():
        S.op("vector", lambda e: e.tensor_scalar(
            out=qcol_sb[:, 3:4], in0=qcol_sb[:, 2:3], scalar1=-192.0, scalar2=None,
            op0=OP.mult), deps=["qscale"], ev="n192sq", sem="s_ve", drain=True)
        for j in range(NCH):
            slot = j % 4
            deps = [("wfload2", j), "n192sq"]
            if j == 0:
                deps.append(("wpconv", NCH - 1))
            S.op("scalar", (lambda jj, sl: lambda e: e.activation(
                out=qtmp_sb, in_=att2_sb[:, sl * D:(sl + 1) * D], func=AF.Identity,
                bias=consts_sb[:, 3:4], scale=qcol_sb[:, 1:2]))(j, slot),
                deps=deps, sem="s_ac", drain=True)
            S.op("scalar", (lambda jj: lambda e: e.activation(
                out=wf_sb[jj], in_=qtmp_sb, func=AF.Identity,
                bias=qcol_sb[:, 3:4], scale=qcol_sb[:, 2:3]))(j),
                ev=("wfqc", j), sem="s_ac", drain=True)
            if j + 4 < NCH:
                emit_wfq_load(j + 4)
        S.op("scalar", lambda e: None, deps=[("wfqc", NCH - 1)], ev=("wfq", NCH - 1),
             sem="s_ac", inc=0, drain=False)
    emit_A1(0)
    emit_wqstage(0, 4)
    emit_A2(0)
    emit_A1(1)
    emit_wqstage(4, NCH)
    for j in range(3):
        emit_wfstat_load(j)
    for t in range(NTILES):
        emit_A3(t)
        ntl = max(1, NTILES // 2)
        if t < ntl:
            for j in range(t * NCH // ntl, (t + 1) * NCH // ntl):
                emit_wfstat(j)
                if j + 3 < NCH:
                    emit_wfstat_load(j + 3)
        if t == ntl and NTILES > 1:
            emit_qscale()
            for j in range(min(4, NCH)):
                emit_wfq_load(j)
        if t + 1 < NTILES:
            emit_A2(t + 1)
        if t + 2 < NTILES:
            emit_A1(t + 2)

    WQFREE = ("qkvmm", NTILES - 1, 23)
    for j in range(NCH):
        slot = j % 3
        deps = [("wfstat", NCH - 1)] if j < 3 else [("wpconv", j - 3)]
        S.op("scalar", (lambda jj, sl: lambda e: e.dma_start(
            out=wstage[:, sl * D:(sl + 1) * D], in_=wprojT[jj * 128:(jj + 1) * 128, :]))(j, slot),
            deps=deps, ev=("wpload", j), sem=f"s_w{slot}", inc=16, drain=False)
        S.op("scalar", (lambda jj, sl: lambda e: e.copy(
            out=wp_sb[jj], in_=wstage[:, sl * D:(sl + 1) * D]))(j, slot),
            deps=[("wpload", j), WQFREE], ev=("wpconv", j), sem="s_ac", drain=True)
    S.op("scalar", lambda e: e.dma_start(
        out=wstage[:, 0:NCH * 16].rearrange("p (j r) -> p j r", j=NCH),
        in_=a_in.rearrange("(j p) r -> p j r", p=128)),
        deps=[("wpconv", NCH - 1)], ev="aload", sem="s_w", inc=16, drain=False)
    for j in range(NCH):
        S.op("scalar", (lambda jj: lambda e: e.copy(
            out=a_sb[jj], in_=wstage[:, jj * 16:(jj + 1) * 16]))(j),
            deps=["aload"], ev=("aconv", j) if j == NCH - 1 else None, sem="s_ac",
            drain=(j == NCH - 1))
    S.op("gpsimd", lambda e: e.memset(onesrow_sb[:], 1.0), ev="onesrowdone", sem="s_gp", drain=True)
    S.op("scalar", lambda e: [
        e.dma_start(out=wstage[0:16, 0:D], in_=b_in[:, :]),
        e.dma_start(out=wstage[0:1, D:2 * D], in_=bpr_in[:, :]),
        e.dma_start(out=wstage[0:1, 2 * D:3 * D], in_=bfr_in[:, :])],
        deps=[("aconv", NCH - 1)], ev="bload", sem="s_w", inc=48, drain=False)
    S.op("scalar", lambda e: e.copy(out=b_sb0, in_=wstage[0:16, 0:512]),
         deps=["bload"], sem="s_ac", drain=True)
    S.op("scalar", lambda e: e.copy(out=b_sb1, in_=wstage[0:16, 512:D]),
         sem="s_ac", drain=True)
    S.op("scalar", lambda e: e.copy(out=bprow_sb[:], in_=wstage[0:1, D:2 * D]),
         ev="bconv", sem="s_ac", drain=True)
    S.op("scalar", lambda e: e.copy(out=bfrow_sb[:], in_=wstage[0:1, 2 * D:3 * D]),
         deps=["onesrowdone"], ev="browconv", sem="s_ac", drain=True)
    if NTILES <= 1:
        emit_qscale()
        for j in range(min(4, NCH)):
            emit_wfq_load(j)
    emit_wfq_convert()

    S.op("vector", lambda e: e.tensor_copy(out=totals_sb[:, 0:8], in_=kcs_sb[:, T_T - 1::T_T]),
         deps=[("scan_k", NTILES - 1, NCH - 1)], sem="s_ve")
    S.op("vector", lambda e: e.tensor_copy(out=totals_sb[:, 8:16], in_=kvcs_sb[:, T_T - 1::T_T]),
         deps=[("scan_kv", NTILES - 1, NCH - 1)], ev="totals", sem="s_ve")
    S.op("gpsimd", lambda e: e.dma_start(out=cc_in[:, :], in_=totals_sb[:]),
         deps=["totals"], ev="ccstage", sem="s_sm", inc=16, drain=False)
    S.op("gpsimd", lambda e: e.collective_compute(
        "AllGather", mybir.AluOpType.bypass, replica_groups=[list(range(NCORES))],
        ins=[cc_in.opt()], outs=[cc_out.opt()]),
        deps=["ccstage"], ev="ccdone", sem="s_cc", inc=1, drain=False)
    S.op("gpsimd", lambda e: [
        e.dma_start(out=allg_sb[:, s * 16:(s + 1) * 16], in_=cc_out[s * 128:(s + 1) * 128, :])
        for s in range(NCORES)],
        deps=["ccdone"], ev="allg", sem="s_sm", inc=16 * NCORES, drain=False)
    S.op("vector", lambda e: e.tensor_scalar(
        out=off_sb[0][:], in0=allg_sb[:, 0:16], scalar1=mask_sb[:, 0:1], scalar2=None,
        op0=OP.mult),
        deps=["allg", "w_small"], sem="s_ve", drain=True)
    for s in range(1, NCORES):
        S.op("vector", (lambda ss: lambda e: e.scalar_tensor_tensor(
            out=off_sb[ss % 2][:], in0=allg_sb[:, ss * 16:(ss + 1) * 16],
            scalar=mask_sb[:, ss:ss + 1], in1=off_sb[(ss - 1) % 2][:],
            op0=OP.mult, op1=OP.add))(s),
            ev="offsets" if s == NCORES - 1 else None, sem="s_ve", drain=True)
    offF = off_sb[(NCORES - 1) % 2]

    ORDER = ([NTILES - 1] + list(range(NTILES - 1))) if NTILES > 1 else [0]
    POS = {tt: p for p, tt in enumerate(ORDER)}
    def prv(tt, k=1):
        p = POS[tt] - k
        return ORDER[p] if p >= 0 else None
    def emit_S1a(t):
        deps = [("spill", NTILES - 2, j) for j in range(NCH)]
        deps += [("denmm", prv(t)), ("y", prv(t), NCH - 1)]
        S.op("sync", (lambda tt: lambda e: [
            e.dma_start(out=d_sb[:].rearrange("p (j c) -> p j c", j=NCH),
                        in_=kcs_hbm[:, :, tt * T_T:(tt + 1) * T_T]),
            e.dma_start(out=tn_sb[:].rearrange("p (j c) -> p j c", j=NCH),
                        in_=kvcs_hbm[:, :, tt * T_T:(tt + 1) * T_T]),
            e.dma_start(out=q1_sb[:].rearrange("p (j c) -> p j c", j=NCH),
                        in_=q1_hbm[:, :, tt * T_T:(tt + 1) * T_T])])(t),
            deps=deps, ev=("csload", t), sem="s_ld", inc=48, drain=False)
        deps = ([("x2", prv(t, 2), NPC - 1)] if prv(t, 2) is not None
                else [("xdoneA", NTILES - 1)])
        S.op("sync", (lambda tt: lambda e: e.dma_start(
            out=x_sb[tt % 2][:].rearrange("p (i c) -> p i c", i=NPC),
            in_=x_hbm[tt * T_T:(tt + 1) * T_T, :].rearrange("(i p) c -> p i c", p=128)))(t),
            deps=deps, ev=("xB", t), sem=f"s_x{t % 2}", inc=16, drain=False)
    def emit_S1c(t):
        res = (t == NTILES - 1)
        if res:
            for j in range(NCH):
                dd = [("add1", t, 3 * j + 2)]
                if j == 0:
                    dd += [("trA", NTILES - 1, NCH - 1), ("qkvmm", NTILES - 1, 23)]
                S.op("vector", (lambda jj: lambda e: e.tensor_tensor(
                    out=d_sb[:, jj * T_T:(jj + 1) * T_T], in0=kcs_sb[:, jj * T_T:(jj + 1) * T_T],
                    in1=q1_sb[:, jj * T_T:(jj + 1) * T_T], op=OP.mult))(j),
                    deps=dd, ev=("d", t, j), sem="s_ve")
                S.op("vector", (lambda jj: lambda e: e.tensor_tensor(
                    out=tn_sb[:, jj * T_T:(jj + 1) * T_T], in0=kvcs_sb[:, jj * T_T:(jj + 1) * T_T],
                    in1=q1_sb[:, jj * T_T:(jj + 1) * T_T], op=OP.mult))(j),
                    ev=("tnl", t, j), sem="s_ve", drain=(j == NCH - 1))
            for j in range(NCH):
                S.op("vector", (lambda jj: lambda e: e.tensor_scalar(
                    out=soff_sb[:, jj * 16:(jj + 1) * 16], in0=selc_sb[:, jj * 16:(jj + 1) * 16],
                    scalar1=offF[:, jj:jj + 1], scalar2=None, op0=OP.mult))(j),
                    deps=["offsets"], ev=("soff", t, j), sem="s_ve")
                S.op("vector", (lambda jj: lambda e: e.scalar_tensor_tensor(
                    out=tn_sb[:, jj * T_T:(jj + 1) * T_T], in0=q1_sb[:, jj * T_T:(jj + 1) * T_T],
                    scalar=offF[:, 8 + jj:9 + jj], in1=tn_sb[:, jj * T_T:(jj + 1) * T_T],
                    op0=OP.mult, op1=OP.add))(j),
                    ev=("tn", t, j), sem="s_ve")
            return
        for j in range(NCH):
            dd = [("csload", t), "offsets"]
            S.op("vector", (lambda jj: lambda e: e.scalar_tensor_tensor(
                out=d_sb[:, jj * T_T:(jj + 1) * T_T], in0=d_sb[:, jj * T_T:(jj + 1) * T_T],
                scalar=offF[:, jj:jj + 1], in1=q1_sb[:, jj * T_T:(jj + 1) * T_T],
                op0=OP.add, op1=OP.mult))(j),
                deps=dd, ev=("d", t, j), sem="s_ve")
            S.op("vector", (lambda jj: lambda e: e.scalar_tensor_tensor(
                out=tn_sb[:, jj * T_T:(jj + 1) * T_T], in0=tn_sb[:, jj * T_T:(jj + 1) * T_T],
                scalar=offF[:, 8 + jj:9 + jj], in1=q1_sb[:, jj * T_T:(jj + 1) * T_T],
                op0=OP.add, op1=OP.mult))(j),
                ev=("tn", t, j), sem="s_ve")
    def emit_S1d(t):
        deps = [("d", t, NCH - 1)] + bank_free.get(4, [])
        if t == ORDER[0]:
            deps += [("wpconv", NCH - 1)]
        if t == NTILES - 1:
            S.op("tensor", lambda e: [
                e.matmul(out=mm_ps[4][0:16, :],
                         lhsT=selc_sb[:, j2 * 16:(j2 + 1) * 16],
                         rhs=d_sb[:, j2 * T_T:(j2 + 1) * T_T],
                         start=(j2 == 0), stop=False) for j2 in range(NCH)],
                deps=deps, sem="s_te")
            S.op("tensor", lambda e: [
                e.matmul(out=mm_ps[4][0:16, :],
                         lhsT=soff_sb[:, j2 * 16:(j2 + 1) * 16],
                         rhs=q1_sb[:, j2 * T_T:(j2 + 1) * T_T],
                         start=False, stop=(j2 == NCH - 1)) for j2 in range(NCH)],
                deps=[("soff", t, NCH - 1)], ev=("denmm", t), sem="s_te")
        else:
            S.op("tensor", lambda e: [
                e.matmul(out=mm_ps[4][0:16, :],
                         lhsT=selc_sb[:, j2 * 16:(j2 + 1) * 16],
                         rhs=d_sb[:, j2 * T_T:(j2 + 1) * T_T],
                         start=(j2 == 0), stop=(j2 == NCH - 1)) for j2 in range(NCH)],
                deps=deps, ev=("denmm", t), sem="s_te")
        bank_free[4] = [("clip", t)]
        S.op("scalar", lambda e: e.activation(out=rbt_sb[:], in_=mm_ps[4][0:16, :],
                                              func=AF.Relu, bias=consts_sb[0:16, 2:3], scale=1.0),
             deps=[("denmm", t)], ev=("clip", t), sem="s_ac", drain=True)
        S.op("scalar", lambda e: e.activation(out=rbt2_sb[:], in_=rbt_sb[:],
                                              func=AF.Ln, bias=consts_sb[0:16, 4:5], scale=1.0),
             sem="s_ac", drain=True)
        S.op("scalar", lambda e: e.activation(out=rb_sb[:], in_=rbt2_sb[:],
                                              func=AF.Exp, bias=consts_sb[0:16, 1:2], scale=-1.0),
             deps=([("bcmm", prv(t), NCH - 1)] if prv(t) is not None else []),
             ev=("rb", t), sem="s_ac")
    def emit_S1b2(t):
        for j in range(NCH):
            bb = 4 + (j % 2)
            deps = [("rb", t)] + bank_free.get(bb, [])
            S.op("tensor", (lambda jj, p: lambda e: e.matmul(
                out=mm_ps[p][:], lhsT=sel2_sb[:, jj * 128:(jj + 1) * 128], rhs=rb_sb[:],
                start=True, stop=True))(j, bb),
                deps=deps, ev=("bcmm", t, j), sem="s_te")
            S.op("scalar", (lambda jj, p: lambda e: e.activation(
                out=rbc_sb[:, jj * T_T:(jj + 1) * T_T], in_=mm_ps[p][:], func=AF.Identity,
                bias=consts_sb[:, 1:2], scale=1.0))(j, bb),
                deps=[("bcmm", t, j)] +
                     ([("scan_k", NTILES - 1, NCH - 1), ("kv", NTILES - 1, NCH - 1)]
                      if (t == ORDER[0] and j == 0) else []) +
                     ([("projmm", prv(t), NPC - 1)] if (prv(t) is not None and j == 0) else []),
                ev=("rbc", t, j), sem="s_ac")
            bank_free[bb] = [("rbc", t, j)]
        for j in range(NCH):
            S.op("vector", (lambda jj: lambda e: e.tensor_tensor(
                out=y_sb[:, jj * T_T:(jj + 1) * T_T], in0=tn_sb[:, jj * T_T:(jj + 1) * T_T],
                in1=rbc_sb[:, jj * T_T:(jj + 1) * T_T], op=OP.mult))(j),
                deps=[("rbc", t, j), ("tn", t, j)],
                ev=("y", t, j), sem="s_ve")
        for i in range(NPC):
            b0, b1 = (0, 1) if i % 2 == 0 else (2, 3)
            deps = [("y", t, NCH - 1)] + bank_free.get(b0, []) + bank_free.get(b1, [])
            if t == 0 and i == 0:
                deps.append("browconv")
            S.op("tensor", (lambda ii, p0, p1: lambda e: [
                e.matmul(out=mm_ps[p0][:],
                         lhsT=y_sb[:, k * T_T + ii * 128:k * T_T + (ii + 1) * 128],
                         rhs=wp_sb[k][:, 0:512], start=(k == 0), stop=False)
                for k in range(NCH)] + [
                e.matmul(out=mm_ps[p0][:], lhsT=onesrow_sb[0:1, 0:128],
                         rhs=bprow_sb[0:1, 0:512], start=False, stop=True)] + [
                e.matmul(out=mm_ps[p1][:],
                         lhsT=y_sb[:, k * T_T + ii * 128:k * T_T + (ii + 1) * 128],
                         rhs=wp_sb[k][:, 512:1024], start=(k == 0), stop=False)
                for k in range(NCH)] + [
                e.matmul(out=mm_ps[p1][:], lhsT=onesrow_sb[0:1, 0:128],
                         rhs=bprow_sb[0:1, 512:1024], start=False, stop=True)])(i, b0, b1),
                deps=deps, ev=("projmm", t, i), sem="s_te")
            bank_free[b0] = [("projev", t, i)]
            bank_free[b1] = [("projev", t, i)]
            att = att_sb2[t % 2]
            S.op("scalar", (lambda ii, aa, p0: lambda e: e.copy(
                out=aa[:, ii * D:ii * D + 512], in_=mm_ps[p0][:]))(i, att, b0),
                deps=[("projmm", t, i)] +
                     ([("add1", NTILES - 1, 23)] if (t == 0 and i == 0) else []) +
                     ([("x3", prv(t, 2), NPC - 1)] if (prv(t, 2) is not None and i == 0) else []),
                sem="s_ac")
            S.op("scalar", (lambda ii, aa, p1: lambda e: e.copy(
                out=aa[:, ii * D + 512:(ii + 1) * D], in_=mm_ps[p1][:]))(i, att, b1),
                ev=("projev", t, i), sem="s_ac")

    def emit_S2a_head(t):
        att = att_sb2[t % 2]
        for i in range(NPC):
            S.op("vector", (lambda ii, aa: lambda e: e.bn_stats(
                out=stat_sb[:, 64 + 0:64 + 6], in_=aa[:, ii * D:ii * D + 512]))(i, att),
                deps=[("projev", t, i)], sem="s_ve")
            S.op("vector", (lambda ii, aa: lambda e: e.bn_stats(
                out=stat_sb[:, 64 + 6:64 + 12], in_=aa[:, ii * D + 512:(ii + 1) * D]))(i, att),
                sem="s_ve", drain=True)
            S.op("vector", (lambda ii: lambda e: e.bn_aggr(
                out=stat_sb[:, 16 + 2 * ii:16 + 2 * ii + 2],
                in_=stat_sb[:, 64:64 + 12]))(i),
                sem="s_ve", drain=True)
        S.op("vector", lambda e: e.tensor_scalar(
            out=stat_sb[:, 24:28], in0=stat_sb[:, 16:24:2], scalar1=-1.0, scalar2=None,
            op0=OP.mult), ev=("bn2", t), sem="s_ve", drain=True)
        S.op("scalar", lambda e: e.activation(out=stat_sb[:, 28:32], in_=stat_sb[:, 17:24:2],
                                              func=AF.Ln, bias=consts_sb[:, 0:1], scale=1.0),
             deps=[("bn2", t)], sem="s_ac", drain=True)
        S.op("scalar", lambda e: e.activation(out=stat_sb[:, 28:32], in_=stat_sb[:, 28:32],
                                              func=AF.Exp, bias=consts_sb[:, 1:2], scale=-0.5),
             ev=("r2", t), sem="s_ac")
    def emit_S2a_tail(t):
        att = att_sb2[t % 2]
        for i in range(NPC):
            S.op("vector", (lambda ii, aa: lambda e: e.tensor_scalar(
                out=ffn_sb[:, ii * D:(ii + 1) * D], in0=aa[:, ii * D:(ii + 1) * D],
                scalar1=stat_sb[:, 24 + ii:25 + ii], scalar2=stat_sb[:, 28 + ii:29 + ii],
                op0=OP.add, op1=OP.mult))(i, att),
                deps=[("r2", t)] +
                     ([("x3", prv(t), NPC - 1)] if (prv(t) is not None and i == 0) else []),
                ev=("ln2", t, i), sem="s_ve")
            deps_x2 = [("ln2", t, i)]
            if t != NTILES - 1:
                deps_x2.append(("xB", t))
            if prv(t) is not None:
                deps_x2.append(("ostore", prv(t)))
            else:
                deps_x2.append(("wfq", NCH - 1))
            S.op("vector", (lambda tt, ii: lambda e: e.tensor_tensor(
                out=x2_sb[:, ii * D:(ii + 1) * D], in0=x_sb[tt % 2][:, ii * D:(ii + 1) * D],
                in1=ffn_sb[:, ii * D:(ii + 1) * D], op=OP.add))(t, i),
                deps=deps_x2, ev=("x2", t, i), sem="s_ve", drain=True)
    def emit_S2b(t):
        att = att_sb2[t % 2]
        for i in range(NPC):
            S.op("vector", (lambda ii: lambda e: e.bn_stats(
                out=stat_sb[:, 64:64 + 6], in_=x2_sb[:, ii * D:ii * D + 512]))(i),
                deps=[("x2", t, i)], sem="s_ve")
            S.op("vector", (lambda ii: lambda e: e.bn_stats(
                out=stat_sb[:, 64 + 6:64 + 12], in_=x2_sb[:, ii * D + 512:(ii + 1) * D]))(i),
                sem="s_ve", drain=True)
            S.op("vector", (lambda ii: lambda e: e.bn_aggr(
                out=stat_sb[:, 32 + 2 * ii:32 + 2 * ii + 2],
                in_=stat_sb[:, 64:64 + 12]))(i),
                sem="s_ve", drain=True)
        S.op("vector", lambda e: e.tensor_scalar(
            out=stat_sb[:, 40:44], in0=stat_sb[:, 32:40:2], scalar1=-1.0, scalar2=None,
            op0=OP.mult), ev=("bn3", t), sem="s_ve", drain=True)
        S.op("scalar", lambda e: e.activation(out=stat_sb[:, 44:48], in_=stat_sb[:, 33:40:2],
                                              func=AF.Ln, bias=consts_sb[:, 0:1], scale=1.0),
             deps=[("bn3", t)], sem="s_ac", drain=True)
        S.op("scalar", lambda e: e.activation(out=stat_sb[:, 44:48], in_=stat_sb[:, 44:48],
                                              func=AF.Exp, bias=consts_sb[:, 1:2], scale=-0.5),
             ev=("r3", t), sem="s_ac")
        for i in range(NPC):
            S.op("vector", (lambda ii: lambda e: e.tensor_scalar(
                out=xl3_sb[:, ii * D:(ii + 1) * D], in0=x2_sb[:, ii * D:(ii + 1) * D],
                scalar1=stat_sb[:, 40 + ii:41 + ii], scalar2=stat_sb[:, 44 + ii:45 + ii],
                op0=OP.add, op1=OP.mult))(i),
                deps=[("r3", t)] +
                     ([("scan_kv", NTILES - 1, NCH - 1)] if (t == ORDER[0] and i == 0) else []),
                ev=("ln3", t, i), sem="s_ve")
        for g in range(NCH // 2):
            deps = [("ln3", t, NPC - 1)] + trb_free.get(g % 2, [])
            S.op("tensor", (lambda gg: lambda e: [
                e.transpose(out=tr_ps[gg % 2][:, h * T_T + i2 * 128:h * T_T + (i2 + 1) * 128],
                            in_=xl3_sb[:, i2 * D + (2 * gg + h) * 128:i2 * D + (2 * gg + h + 1) * 128],
                            identity=ident_sb[:])
                for h in range(2) for i2 in range(NPC)])(g),
                deps=deps, ev=("tr3", t, 2 * g + 1), sem="s_te")
            S.op("scalar", (lambda gg: lambda e: e.copy(
                out=xl3T_sb[:, 2 * gg * T_T:(2 * gg + 2) * T_T], in_=tr_ps[gg % 2][:]))(g),
                deps=[("tr3", t, 2 * g + 1)] +
                     ([("wfq", NCH - 1)] if (t == ORDER[0] and g == 0) else []),
                ev=("xl3Tev", t, 2 * g + 1), sem="s_ac")
            trb_free[g % 2] = [("xl3Tev", t, 2 * g + 1)]
        deps = [("xl3Tev", t, NCH - 1)] + bank_free.get(5, [])
        if t == ORDER[0]:
            deps.append(("aconv", NCH - 1))
        S.op("tensor", lambda e: [
            e.matmul(out=mm_ps[5][:, ii2 * 16:(ii2 + 1) * 16],
                     lhsT=xl3T_sb[:, k * T_T + ii2 * 128:k * T_T + (ii2 + 1) * 128],
                     rhs=a_sb[k][:], start=(k == 0), stop=(k == NCH - 1))
            for ii2 in range(NPC) for k in range(NCH)],
            deps=deps, ev=("xamm", t), sem="s_te")
        bank_free[5] = [("xaev", t)]
        S.op("scalar", lambda e: e.copy(out=xa_sb[:], in_=mm_ps[5][:, 0:NPC * 16]),
             deps=[("xamm", t)], ev=("xaev", t), sem="s_ac")
        deps_xat = [("xaev", t)] + trb_free.get(0, [])
        S.op("tensor", lambda e: [
            e.transpose(out=tr_ps[0][0:16, ii2 * 128:(ii2 + 1) * 128],
                        in_=xa_sb[:, ii2 * 16:(ii2 + 1) * 16], identity=ident_sb[:])
            for ii2 in range(NPC)],
            deps=deps_xat, ev=("xaT", t), sem="s_te")
        trb_free[0] = [("xaTev", t)]
        S.op("scalar", lambda e: e.copy(out=xaT_sb, in_=tr_ps[0][0:16, 0:NPC * 128]),
             deps=[("xaT", t)], ev=("xaTev", t), sem="s_ac")
        for i in range(NPC):
            f0, f1 = (2, 3) if i % 2 == 0 else (0, 1)
            deps_f = [("xaTev", t)] + bank_free.get(f0, []) + bank_free.get(f1, [])
            if t == ORDER[0] and i == 0:
                deps_f += ["bconv", ("wfq", NCH - 1)]
            S.op("tensor", (lambda ii, p0, p1: lambda e: [
                e.matmul(out=mm_ps[p0][:],
                         lhsT=xl3T_sb[:, k * T_T + ii * 128:k * T_T + (ii + 1) * 128],
                         rhs=wf_sb[k][:, 0:512], start=(k == 0), stop=False)
                for k in range(NCH)] + [
                e.matmul(out=mm_ps[p0][:], lhsT=xaT_sb[:, ii * 128:(ii + 1) * 128],
                         rhs=b_sb0, start=False, stop=False)] + [
                e.matmul(out=mm_ps[p0][:], lhsT=onesrow_sb[0:1, 0:128],
                         rhs=bfrow_sb[0:1, 0:512], start=False, stop=True)] + [
                e.matmul(out=mm_ps[p1][:],
                         lhsT=xl3T_sb[:, k * T_T + ii * 128:k * T_T + (ii + 1) * 128],
                         rhs=wf_sb[k][:, 512:1024], start=(k == 0), stop=False)
                for k in range(NCH)] + [
                e.matmul(out=mm_ps[p1][:], lhsT=xaT_sb[:, ii * 128:(ii + 1) * 128],
                         rhs=b_sb1, start=False, stop=False)] + [
                e.matmul(out=mm_ps[p1][:], lhsT=onesrow_sb[0:1, 0:128],
                         rhs=bfrow_sb[0:1, 512:1024], start=False, stop=True)])(i, f0, f1),
                deps=deps_f, ev=("ffnmm", t, i), sem="s_te")
            bank_free[f0] = [("ffnev", t, i)]
            bank_free[f1] = [("ffnev", t, i)]
            S.op("scalar", (lambda ii, p0: lambda e: e.copy(
                out=ffn_sb[:, ii * D:ii * D + 512], in_=mm_ps[p0][:]))(i, f0),
                deps=[("ffnmm", t, i)] +
                     ([("x2", t, NPC - 1)] if i == 0 else []),
                sem="s_ac")
            S.op("scalar", (lambda ii, p1: lambda e: e.copy(
                out=ffn_sb[:, ii * D + 512:(ii + 1) * D], in_=mm_ps[p1][:]))(i, f1),
                ev=("ffnev", t, i), sem="s_ac")
        for i in range(NPC):
            S.op("vector", (lambda ii: lambda e: e.bn_stats(
                out=stat_sb[:, 64:64 + 6], in_=ffn_sb[:, ii * D:ii * D + 512]))(i),
                deps=[("ffnev", t, i)], sem="s_ve")
            S.op("vector", (lambda ii: lambda e: e.bn_stats(
                out=stat_sb[:, 64 + 6:64 + 12], in_=ffn_sb[:, ii * D + 512:(ii + 1) * D]))(i),
                sem="s_ve", drain=True)
            S.op("vector", (lambda ii: lambda e: e.bn_aggr(
                out=stat_sb[:, 48 + 2 * ii:48 + 2 * ii + 2],
                in_=stat_sb[:, 64:64 + 12]))(i),
                sem="s_ve", drain=True)
        S.op("vector", lambda e: e.tensor_scalar(
            out=stat_sb[:, 56:60], in0=stat_sb[:, 48:56:2], scalar1=-1.0, scalar2=None,
            op0=OP.mult), ev=("bn4", t), sem="s_ve", drain=True)
        S.op("scalar", lambda e: e.activation(out=stat_sb[:, 60:64], in_=stat_sb[:, 49:56:2],
                                              func=AF.Ln, bias=consts_sb[:, 0:1], scale=1.0),
             deps=[("bn4", t)], sem="s_ac", drain=True)
        S.op("scalar", lambda e: e.activation(out=stat_sb[:, 60:64], in_=stat_sb[:, 60:64],
                                              func=AF.Exp, bias=consts_sb[:, 1:2], scale=-0.5),
             ev=("r4", t), sem="s_ac")
        for i in range(NPC):
            S.op("vector", (lambda ii, aa: lambda e: e.tensor_scalar(
                out=aa[:, ii * D:(ii + 1) * D], in0=ffn_sb[:, ii * D:(ii + 1) * D],
                scalar1=stat_sb[:, 56 + ii:57 + ii], scalar2=stat_sb[:, 60 + ii:61 + ii],
                op0=OP.add, op1=OP.mult))(i, att),
                deps=[("r4", t)], ev=("ln4", t, i), sem="s_ve")
            S.op("gpsimd", (lambda ii, aa: lambda e: e.tensor_tensor(
                out=x2_sb[:, ii * D:(ii + 1) * D], in0=x2_sb[:, ii * D:(ii + 1) * D],
                in1=aa[:, ii * D:(ii + 1) * D], op=OP.add))(i, att),
                deps=[("ln4", t, i)], ev=("x3", t, i), sem="s_gp")
        S.op("sync", (lambda tt: lambda e: e.dma_start(
            out=out_hbm[tt * T_T:(tt + 1) * T_T, :].rearrange("(i p) c -> p i c", p=128),
            in_=x2_sb[:].rearrange("p (i c) -> p i c", i=NPC)))(t),
            deps=[("x3", t, i3) for i3 in range(NPC)], ev=("ostore", t), sem="s_o", inc=16,
            drain=False)

    emit_S1c(ORDER[0])
    emit_S1d(ORDER[0])
    emit_S1b2(ORDER[0])
    if NTILES > 1:
        emit_S1a(ORDER[1])
        emit_S1c(ORDER[1])
    for p, t in enumerate(ORDER):
        if p + 1 < NTILES:
            emit_S1d(ORDER[p + 1])
        emit_S2a_head(t)
        if p + 1 < NTILES:
            emit_S1b2(ORDER[p + 1])
        emit_S2a_tail(t)
        if p + 2 < NTILES:
            emit_S1a(ORDER[p + 2])
            emit_S1c(ORDER[p + 2])
        emit_S2b(t)

    if trunc is not None:
        S.ops = S.ops[:trunc]
    events = S.resolve()
    engines = {}
    sems = {}
    for s in ["s_x0", "s_x1", "s_w", "s_w0", "s_w1", "s_w2", "s_w3", "s_st", "s_ld", "s_o",
              "s_cc", "s_te", "s_ve", "s_ac", "s_gp", "s_sm",
              "s_st0", "s_st1", "s_st2", "s_st3", "s_st4", "s_st5", "s_st6", "s_st7"]:
        sems[s] = stack.enter_context(nc.semaphore(s))
    block = stack.enter_context(nc.Block())

    def emit_engine(ename, eng):
        sub = Sched()
        sub.ops = [o for o in S.ops if o[0] == ename]
        sub.emit({ename: eng}, sems, events)
        if ename == "sync":
            final_t = (NTILES - 2) if NTILES > 1 else 0
            if ("ostore", final_t) in events:
                final_sem, final_val = events[("ostore", final_t)]
                eng.wait_ge(sems[final_sem], final_val)
            else:
                for sname in ("s_o", "s_x0", "s_x1", "s_st", "s_ld", "s_w", "s_sm"):
                    mx = 0
                    for (evv, (sm, val)) in events.items():
                        if sm == sname:
                            mx = max(mx, val)
                    if mx:
                        eng.wait_ge(sems[sname], mx)

    @block.sync
    def _(sync):
        emit_engine("sync", sync)

    @block.scalar
    def _(scalar):
        emit_engine("scalar", scalar)

    @block.vector
    def _(vector):
        emit_engine("vector", vector)

    @block.tensor
    def _(tensor):
        emit_engine("tensor", tensor)

    @block.gpsimd
    def _(gpsimd):
        emit_engine("gpsimd", gpsimd)

    stack.__exit__(None, None, None)
    return nc


_CACHE = {}


def kernel(L0_attn_x, L0_attn_Wqkv, L0_attn_bqkv, L0_attn_Wproj, L0_attn_bproj,
           L0_attn_scale, L0_ffn_kernel, L0_ffn_bias, L0_ffn_a, L0_ffn_b):
    x = np.ascontiguousarray(np.asarray(L0_attn_x, dtype=np.float32).reshape(2 * 16384, D))
    wqkvT = np.ascontiguousarray(np.asarray(L0_attn_Wqkv, np.float32).T)
    wprojT = np.ascontiguousarray(np.asarray(L0_attn_Wproj, np.float32).T)
    wffn = np.ascontiguousarray(np.asarray(L0_ffn_kernel, np.float32))
    a_np = np.ascontiguousarray(np.asarray(L0_ffn_a, np.float32))
    b_np = np.ascontiguousarray(np.asarray(L0_ffn_b, np.float32))
    bqkv = np.asarray(L0_attn_bqkv, np.float32)
    bproj = np.asarray(L0_attn_bproj, np.float32)
    bffn = np.asarray(L0_ffn_bias, np.float32)
    scale = np.float32(np.asarray(L0_attn_scale, np.float32))

    bqkv_t = np.ascontiguousarray(bqkv.reshape(24, 128).T)
    bpr_rep = np.ascontiguousarray(bproj[None, :])
    bfr_rep = np.ascontiguousarray(bffn[None, :])
    srep = np.full((128, 1), scale, np.float32)
    ident = np.eye(128, dtype=np.float32).astype(bfdt)
    selc = np.zeros((128, NCH * 16), np.float32)
    for j in range(NCH):
        selc[0:64, j * 16 + 2 * j] = 1.0
        selc[64:128, j * 16 + 2 * j + 1] = 1.0
    sel2 = np.zeros((16, NCH * 128), np.float32)
    for j in range(NCH):
        sel2[2 * j, j * 128:j * 128 + 64] = 1.0
        sel2[2 * j + 1, j * 128 + 64:(j + 1) * 128] = 1.0
    selc = selc.astype(bfdt)
    sel2 = sel2.astype(bfdt)

    key = "prog"
    if key not in _CACHE:
        _CACHE[key] = build_nc()
    nc = _CACHE[key]

    in_maps = []
    for c in range(NCORES):
        seq = c // 4
        pos = c % 4
        mask = np.zeros(NCORES, np.float32)
        for j in range(NCORES):
            if j // 4 == seq and (j % 4) < pos:
                mask[j] = 1.0
        mask_rep = np.ascontiguousarray(np.broadcast_to(mask[None, :], (128, NCORES)))
        in_maps.append({
            "x": np.ascontiguousarray(x[c * T_CORE:(c + 1) * T_CORE]),
            "wqkvT": wqkvT, "wprojT": wprojT, "wffn": wffn,
            "a_in": a_np, "b_in": b_np,
            "bqkv": bqkv_t, "bproj_rep": bpr_rep, "bffn_rep": bfr_rep,
            "scale_rep": srep, "ident": ident, "selc": selc, "sel2": sel2,
            "mask": mask_rep,
        })
    res = run_bass_kernel_spmd(nc, in_maps, core_ids=list(range(NCORES)))
    out = np.concatenate([r["out"] for r in res.results], axis=0)
    return out.reshape(2, 16384, D)
